# version 12
# baseline (speedup 1.0000x reference)
import numpy as np
import concourse.bass as bass
import concourse.mybir as mybir
from concourse.bass_utils import run_bass_kernel_spmd

F32 = mybir.dt.float32
BF16 = mybir.dt.bfloat16
I32 = mybir.dt.int32
AF = mybir.ActivationFunctionType
ALU = mybir.AluOpType

EPS = 1e-5
MAGIC = 0x5F3759E0


class Cfg:
    def __init__(self, D=1024, FF=4096, TOK=4096, SLAB=512, HD=64):
        self.D, self.FF, self.TOK, self.SLAB, self.HD = D, FF, TOK, SLAB, HD
        self.CH = 128
        self.SUB = 32
        self.NT = D // 128
        self.NS = TOK // SLAB
        self.CPS = SLAB // self.CH
        self.FFH = FF // 2


class Arena:
    def __init__(self, tensor, width):
        self.t, self.width = tensor, width
        self.free = [(0, width)]
        self.live = {}
        self.layout = {}
        self.hwm = 0

    ALIGN = 16

    def alloc(self, name, ncols):
        assert name not in self.live, name
        A = self.ALIGN
        cands = []
        for idx, (off, sz) in enumerate(self.free):
            aoff = (off + A - 1) // A * A
            pad = aoff - off
            if sz >= ncols + pad:
                cands.append((sz, idx, aoff, pad))
        cands.sort()
        for _, idx, aoff, pad in cands[:1]:
            off, sz = self.free[idx]
            self.free[idx] = (aoff + ncols, sz - pad - ncols)
            if self.free[idx][1] == 0:
                self.free.pop(idx)
            if pad:
                self.free.append((off, pad))
                self.free.sort()
            self.live[name] = (aoff, ncols)
            self.layout[name] = (aoff, ncols)
            self.hwm = max(self.hwm, aoff + ncols)
            return
        raise RuntimeError(f"arena OOM: {name} ({ncols}); free={self.free}")

    def release(self, name):
        off, sz = self.live.pop(name)
        self.layout.setdefault(name, (off, sz))
        self.free.append((off, sz))
        self.free.sort()
        merged = []
        for off, sz in self.free:
            if merged and merged[-1][0] + merged[-1][1] == off:
                merged[-1][1] += sz
            else:
                merged.append([off, sz])
        self.free = [(a, b) for a, b in merged]


def build_program(cfg: Cfg, arena_cols=52200, debug=False, init_only=False):
    c = cfg
    D, FF, TOK, SLAB, CH, SUB = c.D, c.FF, c.TOK, c.SLAB, c.CH, c.SUB
    NT, NS, CPS, HD, FFH = c.NT, c.NS, c.CPS, c.HD, c.FFH
    HPT = 128 // HD
    CHW = NT * CH
    NOH = FFH // 128
    NKH = FFH // 128
    NB = CH // SUB
    VH = min(512, D)
    NVH = D // VH

    nc = bass.Bass("TRN2", target_bir_lowering=False, detect_race_conditions=False)

    xT = nc.dram_tensor("xT", [D, TOK], F32, kind="ExternalInput").ap()
    wkT = nc.dram_tensor("wkT", [D, D], F32, kind="ExternalInput").ap()
    wvT = nc.dram_tensor("wvT", [D, D], F32, kind="ExternalInput").ap()
    wrT = nc.dram_tensor("wrT", [D, D], F32, kind="ExternalInput").ap()
    woT = nc.dram_tensor("woT", [D, D], F32, kind="ExternalInput").ap()
    decf = nc.dram_tensor("decf", [128, NT], F32, kind="ExternalInput").ap()
    tff = nc.dram_tensor("tff", [128, NT], F32, kind="ExternalInput").ap()
    gamf = nc.dram_tensor("gamf", [128, NT], F32, kind="ExternalInput").ap()
    betf = nc.dram_tensor("betf", [128, NT], F32, kind="ExternalInput").ap()
    b1f = nc.dram_tensor("b1f", [128, FF // 128], F32, kind="ExternalInput").ap()
    b2f = nc.dram_tensor("b2f", [128, NT], F32, kind="ExternalInput").ap()
    q1 = nc.dram_tensor("q1", [D, FF], I32, kind="ExternalInput").ap()
    q2 = nc.dram_tensor("q2", [FF, D], I32, kind="ExternalInput").ap()
    s1r = nc.dram_tensor("s1r", [D // 128, 128, FF], F32, kind="ExternalInput").ap()
    s2r = nc.dram_tensor("s2r", [FF // 128, 128, D], F32, kind="ExternalInput").ap()
    flagf = nc.dram_tensor("flagf", [128, 1], F32, kind="ExternalInput").ap()
    outT = nc.dram_tensor("outT", [D, TOK], F32, kind="ExternalOutput").ap()

    k_sc = nc.dram_tensor("k_sc", [D, TOK], BF16).ap()
    r_sc = nc.dram_tensor("r_sc", [D, TOK], BF16).ap()
    v_sc = nc.dram_tensor("v_sc", [TOK, D], BF16).ap()
    b_sc = nc.dram_tensor("b_sc", [TOK, D], BF16).ap()
    w1_sc = nc.dram_tensor("w1_sc", [D, FF], BF16).ap()
    w2_sc = nc.dram_tensor("w2_sc", [FF, D], BF16).ap()
    x1_sc = nc.dram_tensor("x1_sc", [D, TOK], F32).ap()
    h_sc = nc.dram_tensor("h_sc", [D, TOK], BF16).ap()
    z1_sc = nc.dram_tensor("z1_sc", [D, TOK], F32).ap()
    ccin = nc.dram_tensor("ccin", [128, NT * HD], F32).ap()
    ccout = nc.dram_tensor("ccout", [2, 128, NT * HD], F32).ap()
    dbg = {}
    if debug:
        for nm, shape, dt in [("d_k", [D, TOK], BF16), ("d_r", [D, TOK], BF16),
                              ("d_v", [TOK, D], BF16), ("d_b", [TOK, D], BF16),
                              ("d_x1", [D, TOK], F32), ("d_h", [D, TOK], BF16),
                              ("d_z1", [D, TOK], F32), ("d_cc", [128, NT * HD], F32),
                              ("d_w1", [D, FF], BF16), ("d_w2", [FF, D], BF16),
                              ("d_pre", [D, TOK], F32), ("d_yb", [D, TOK], BF16),
                              ("d_at", [128, 4 * 128], BF16),
                              ("d_scn", [128, 128], F32),
                              ("d_rep", [128, 128], F32),
                              ("d_zf", [128, 128], F32),
                              ("d_dkb", [128, NT * CH], BF16),
                              ("d_dqb", [128, 4 * NT * CH], BF16),
                              ("d_d1b", [128, NT * CH], BF16),
                              ("d_tbf", [128, NT * 128], BF16),
                              ("d_qh", [128, 4 * 1024], BF16),
                              ("d_kd", [128, 1024], BF16),
                              ("d_qt", [128, 1024], BF16),
                              ("d_sb", [128, 1024], BF16)]:
            dbg[nm] = nc.dram_tensor(nm, shape, dt, kind="ExternalOutput").ap()

    def fm(dram_ap, col0, ncols):
        return dram_ap[:, col0:col0 + ncols].rearrange("(i p) c -> p i c", p=128)

    def tm(dram_ap, row0, nrows):
        return dram_ap[row0:row0 + nrows, :].rearrange("(cc t) n -> t cc n", t=128)

    sem_names = [
        "gx", "gw", "gq", "yq", "yq2", "gcs", "gsin", "gwo",
        "tkA", "vkA", "trA", "srA", "tvA", "vvA", "tbA", "vbA", "tuA", "vsA",
        "yst", "cc", "vPr",
        "sld", "sldKR", "sldVB", "tA1", "sAe", "vA1", "tY", "sY", "tU1", "vS1",
        "tWo", "vPre", "tE", "vLN", "sSq", "yx1",
        "sld2", "tF1", "sF1", "tF2", "vF2", "vLN2", "sSq2", "yz1", "yh",
        "sld3", "tG1", "sG1", "tG2", "vG2", "vLN3", "yout", "ph",
    ]

    from contextlib import ExitStack
    with ExitStack() as ctx:
        E = ctx.enter_context
        ar_t = E(nc.sbuf_tensor("arena", [128, arena_cols], F32))
        ar = Arena(ar_t, arena_cols)
        sems = {n: E(nc.semaphore(n)) for n in sem_names}
        pb = [E(nc.psum_tensor(f"pb{i}", [128, 512], F32)) for i in range(6)]

        A = {}

        def al(name, cols):
            ar.alloc(name, cols)

        def ap(name):
            off, sz = ar.layout[name]
            return ar_t[:, off:off + sz]

        def bf(name):
            return ap(name).bitcast(BF16)

        def iv(name):
            return ap(name).bitcast(I32)

        def fm3(apx, n):
            return apx.rearrange("p (i c) -> p i c", i=n)

        for nm, ncol in [("dec", NT), ("tf", NT), ("gam", NT), ("bet", NT),
                         ("b2", NT), ("b1", FF // 128), ("flag", 1),
                         ("dec128", NT), ("rdec", NT), ("sA", 2 * NT),
                         ("zf", 128), ("scn", 128), ("rep", 128),
                         ("iotaf", 128),
                         ("D1f", CHW), ("D1b", CHW // 2), ("Dkb", CHW // 2),
                         ("Dqb", 4 * CHW // 2), ("TBf", NT * 128 // 2),
                         ("trimask", CH // 2), ("identb", 64), ("identf", 128),
                         ("ones", 64),
                         ("Sf0", NT * HD), ("Sf1", NT * HD), ("Sin", NT * HD)]:
            al(nm, ncol)
        for nm, ncol in [("wk", NT * D // 2), ("wv", NT * D // 2),
                         ("wr", NT * D // 2), ("xb", NT * SLAB // 2),
                         ("xb2", NT * SLAB // 2),
                         ("kf", NT * SLAB // 2), ("rf", NT * SLAB // 2),
                         ("vt", CPS * D // 2), ("Bt", CPS * D // 2),
                         ("dq_q", 1024), ("dq_s", 1024), ("dq_w", 512)]:
            al(nm, ncol)

        def restructure(tag):
            if tag == "B1":
                for nm in ["wk", "wv", "wr", "xb", "xb2"]:
                    ar.release(nm)
                for nm, ncol in [("x1", NT * SLAB),
                                 ("pbm", NT * SLAB // 2), ("sq", NT * SLAB // 2),
                                 ("mu", SLAB), ("istd", SLAB), ("t_a", SLAB),
                                 ("t_b", SLAB), ("t_c", SLAB),
                                 ("wo", NT * D // 2), ("xf", NT * SLAB),
                                 ("yb", NT * SLAB // 2), ("pre1", NT * SLAB),
                                 ("qt", CHW // 2),
                                 ("Qh", 4 * CHW // 2), ("kd", CHW // 2),
                                 ("At", HPT * CH), ("Sb", CHW // 2)]:
                    al(nm, ncol)
            elif tag == "B2h0":
                for nm in ["wo", "kf", "rf", "vt", "Bt", "yb", "pre1",
                           "qt", "Qh", "kd", "At", "Sb", "xf",
                           "dq_q", "dq_s", "dq_w",
                           "D1f", "D1b", "Dkb", "Dqb", "TBf", "trimask",
                           "identb", "identf", "Sf0", "Sf1", "Sin",
                           "scn", "rep", "zf", "iotaf", "sA"]:
                    ar.release(nm)
                items = [("w1h", NT * FFH // 2), ("w2h", NKH * D // 2),
                         ("hb", NT * SLAB // 2), ("ab", NOH * SLAB // 2),
                         ("z1", NT * SLAB), ("z1b", NT * SLAB),
                         ("x1b", NT * SLAB), ("tz", SLAB), ("tz2", SLAB)]
                for nm, ncol in sorted(items, key=lambda t: -t[1]):
                    al(nm, ncol)
                ar.layout["m0"] = ar.layout["z1"]
                ar.layout["m1"] = ar.layout["z1b"]
                if ar.layout["ab"][1] >= NT * SLAB:
                    ar.layout["outb"] = (ar.layout["ab"][0], NT * SLAB)
                else:
                    al("outb", NT * SLAB)
            elif tag == "B2h1":
                pass

        restructure("B1")
        restructure("B2h0")
        restructure("B2h1")

        def walk(me, eng):
            cnt = {n: 0 for n in sem_names}

            def op(name, fn, waits=(), inc=None, amt=1, drain=False):
                if name == me:
                    for sname, val in waits:
                        if val > 0:
                            eng.wait_ge(sems[sname], val)
                    instr = fn(eng)
                    if inc is not None:
                        instr.then_inc(sems[inc], amt)
                    if drain:
                        eng.drain()
                if inc is not None:
                    cnt[inc] += amt
                return cnt[inc] if inc else None

            def opd(name, fn, waits=(), inc=None, amt=1):
                return op(name, fn, waits=waits, inc=inc, amt=amt, drain=True)

            def barrier():
                target = cnt["ph"] + 5
                for en in ("gpsimd", "tensor", "vector", "scalar", "sync"):
                    if en == me:
                        eng.drain()
                        eng.nop().then_inc(sems["ph"], 1)
                    cnt["ph"] += 1
                eng.wait_ge(sems["ph"], target)

            g_, t_, v_, s_, y_ = "gpsimd", "tensor", "vector", "scalar", "sync"

            def ln_block(src, dst, dst_bf, gamma, beta, eps, tEs, vLNs, sSqs,
                         src_waits, e1b=2, e2b=3):
                invD = 1.0 / D
                w0 = list(src_waits)
                if cnt[tEs] > 0:
                    w0.append((tEs, cnt[tEs]))
                for i in range(NT):
                    op(v_, lambda e, i=i: e.tensor_copy(
                        bf("pbm")[:, i * SLAB:(i + 1) * SLAB],
                        ap(src)[:, i * SLAB:(i + 1) * SLAB]),
                       waits=(w0 if i == 0 else ()), inc=vLNs)
                    op(s_, lambda e, i=i: e.activation(
                        bf("sq")[:, i * SLAB:(i + 1) * SLAB],
                        ap(src)[:, i * SLAB:(i + 1) * SLAB], AF.Square),
                       waits=(w0 if i == 0 else ()), inc=sSqs)
                mir_done, sq_done = cnt[vLNs], cnt[sSqs]
                for srcbuf, bank, wsem, wval in (("pbm", e1b, vLNs, mir_done),
                                                 ("sq", e2b, sSqs, sq_done)):
                    for k in range(NT):
                        last = k == NT - 1
                        op(t_, lambda e, k=k, bank=bank, srcbuf=srcbuf,
                           last=last: e.matmul(
                               pb[bank][:, 0:SLAB], bf("ones"),
                               bf(srcbuf)[:, k * SLAB:(k + 1) * SLAB],
                               start=(k == 0), stop=last),
                           waits=([(wsem, wval)] if k == 0 else ()),
                           inc=(tEs if last else None))
                op(v_, lambda e: e.tensor_scalar(
                    ap("mu"), pb[e1b][:, 0:SLAB], invD, None, ALU.mult, ALU.bypass),
                   waits=[(tEs, cnt[tEs])], inc=vLNs)
                op(v_, lambda e, eps=eps: e.tensor_scalar(
                    ap("t_a"), pb[e2b][:, 0:SLAB], invD, eps, ALU.mult, ALU.add),
                   inc=vLNs)
                op(v_, lambda e: e.scalar_tensor_tensor(
                    ap("t_b"), ap("mu"), -1.0, ap("mu"), ALU.mult, ALU.mult),
                   inc=vLNs)
                op(v_, lambda e: e.tensor_tensor(
                    ap("t_c"), ap("t_a"), ap("t_b"), ALU.add), inc=vLNs)
                op(v_, lambda e: e.tensor_scalar(
                    iv("t_a"), iv("t_c"), 1, -1,
                    ALU.arith_shift_right, ALU.bitwise_xor), inc=vLNs)
                op(v_, lambda e: e.tensor_scalar(
                    iv("t_b"), iv("t_a"), MAGIC, None, ALU.add, ALU.bypass),
                   inc=vLNs)
                curb, nxtb = "t_b", "istd"
                for _ in range(3):
                    op(v_, lambda e, curb=curb: e.tensor_tensor(
                        ap("t_a"), ap(curb), ap(curb), ALU.mult), inc=vLNs)
                    op(v_, lambda e: e.scalar_tensor_tensor(
                        ap("t_a"), ap("t_a"), -0.5, ap("t_c"),
                        ALU.mult, ALU.mult), inc=vLNs)
                    op(v_, lambda e: e.tensor_scalar(
                        ap("t_a"), ap("t_a"), 1.5, None, ALU.add, ALU.bypass),
                       inc=vLNs)
                    op(v_, lambda e, curb=curb, nxtb=nxtb: e.tensor_tensor(
                        ap(nxtb), ap(curb), ap("t_a"), ALU.mult), inc=vLNs)
                    curb, nxtb = nxtb, curb
                assert curb == "istd"
                for i in range(NT):
                    op(v_, lambda e, i=i: e.tensor_tensor(
                        ap("t_b"), ap(src)[:, i * SLAB:(i + 1) * SLAB],
                        ap("mu"), ALU.subtract), inc=vLNs)
                    if dst is not None:
                        op(v_, lambda e, i=i: e.tensor_tensor(
                            ap(dst)[:, i * SLAB:(i + 1) * SLAB], ap("t_b"),
                            ap("istd"), ALU.mult), inc=vLNs)
                    else:
                        op(v_, lambda e: e.tensor_tensor(
                            ap("t_a"), ap("t_b"), ap("istd"), ALU.mult),
                           inc=vLNs)
                        op(v_, lambda e, i=i: e.tensor_scalar(
                            bf(dst_bf)[:, i * SLAB:(i + 1) * SLAB], ap("t_a"),
                            ap(gamma)[:, i:i + 1], ap(beta)[:, i:i + 1],
                            ALU.mult, ALU.add), inc=vLNs)

            for nm, src in [("dec", decf), ("tf", tff), ("gam", gamf),
                            ("bet", betf), ("b2", b2f), ("b1", b1f),
                            ("flag", flagf)]:
                op(g_, lambda e, nm=nm, src=src: e.dma_start(out=ap(nm), in_=src),
                   inc="gw", amt=16)
            for nm, src in [("wk", wkT), ("wv", wvT), ("wr", wrT)]:
                op(g_, lambda e, nm=nm, src=src: e.dma_start(
                    out=fm3(bf(nm), NT), in_=src.rearrange("(k p) n -> p k n", p=128)),
                   inc="gw", amt=16)
            gw_all = cnt["gw"]
            op(g_, lambda e: e.iota(ap("iotaf"), pattern=[[1, 128]], base=0,
                                    channel_multiplier=-1,
                                    allow_small_or_imprecise_dtypes=True),
               inc="gcs")
            gi = cnt["gcs"]

            IW = [("gw", gw_all), ("gcs", gi)]
            opd(v_, lambda e: e.memset(ap("zf"), 0.0), waits=IW, inc="vPr")
            opd(v_, lambda e: e.memset(bf("ones"), 1.0), inc="vPr")
            opd(v_, lambda e: e.memset(ap("Sf0"), 0.0), inc="vPr")
            opd(v_, lambda e: e.memset(ap("Sf1"), 0.0), inc="vPr")
            opd(v_, lambda e: e.memset(pb[5][:, 0:512], 0.0), inc="vPr")
            opd(v_, lambda e: e.tensor_scalar(
                bf("trimask"), ap("iotaf"), 0.0, None, ALU.is_ge, ALU.bypass),
               inc="vPr")
            opd(v_, lambda e: e.tensor_scalar(
                bf("identb"), ap("iotaf"), 0.0, None, ALU.is_equal, ALU.bypass),
               inc="vPr")
            opd(v_, lambda e: e.tensor_scalar(
                ap("identf"), ap("iotaf"), 0.0, None, ALU.is_equal, ALU.bypass),
               inc="vPr")
            for i in range(NT):
                opd(v_, lambda e, i=i: e.tensor_scalar(
                    ap("rep"), ap("zf"), ap("dec")[:, i:i + 1], None,
                    ALU.add, ALU.bypass), inc="vPr")
                opd(v_, lambda e, i=i: e.tensor_tensor_scan(
                    ap("D1f")[:, i * CH:(i + 1) * CH], ap("rep")[:, 0:CH],
                    ap("zf")[:, 0:CH], 1.0, ALU.mult, ALU.add), inc="vPr")
            opd(v_, lambda e: e.tensor_copy(bf("D1b"), ap("D1f")), inc="vPr")
            for i in range(NT):
                opd(v_, lambda e, i=i: e.tensor_copy(
                    ap("dec128")[:, i:i + 1],
                    ap("D1f")[:, i * CH + CH - 1:i * CH + CH]), inc="vPr")
            opd(v_, lambda e: e.reciprocal(ap("rdec"), ap("dec")), inc="vPr")
            opd(v_, lambda e: e.reciprocal(
                ap("sA")[:, 0:NT],
                ap("D1f").rearrange("p (i c) -> p i c", i=NT)[:, :, 16:17]
                .rearrange("p i c -> p (i c)")), inc="vPr")
            HS = SUB // 2
            for i in range(NT):
                opd(v_, lambda e, i=i: e.tensor_scalar(
                    ap("rep"), ap("zf"), ap("dec")[:, i:i + 1], None,
                    ALU.add, ALU.bypass), inc="vPr")
                opd(v_, lambda e: e.tensor_tensor_scan(
                    ap("scn")[:, 0:HS], ap("rep")[:, 0:HS],
                    ap("zf")[:, 0:HS], 1.0, ALU.mult, ALU.add), inc="vPr")
                opd(v_, lambda e, i=i: e.tensor_scalar(
                    ap("rep"), ap("zf"), ap("rdec")[:, i:i + 1], None,
                    ALU.add, ALU.bypass), inc="vPr")
                opd(v_, lambda e: e.tensor_tensor_scan(
                    ap("scn")[:, HS:2 * HS], ap("rep")[:, 0:HS],
                    ap("zf")[:, 0:HS], 1.0, ALU.mult, ALU.add), inc="vPr")
                opd(v_, lambda e: e.tensor_copy(
                    ap("rep")[:, 0:HS], ap("scn")[:, HS - 1::-1]), inc="vPr")
                opd(v_, lambda e: e.memset(ap("rep")[:, HS:HS + 1], 1.0),
                   inc="vPr")
                opd(v_, lambda e: e.tensor_copy(
                    ap("rep")[:, HS + 1:SUB],
                    ap("scn")[:, HS:2 * HS - 1]), inc="vPr")
                opd(v_, lambda e, i=i: e.tensor_scalar(
                    ap("scn")[:, 64:64 + SUB], ap("rep")[:, 0:SUB],
                    ap("tf")[:, i:i + 1], None, ALU.mult, ALU.bypass),
                   inc="vPr")
                for rep_i in range(NB):
                    opd(v_, lambda e, i=i, rep_i=rep_i: e.tensor_copy(
                        bf("Dkb")[:, i * CH + rep_i * SUB:
                                  i * CH + (rep_i + 1) * SUB],
                        ap("scn")[:, 64:64 + SUB]), inc="vPr")
                opd(v_, lambda e: e.tensor_copy(
                    ap("rep")[:, 64:64 + HS], ap("scn")[:, 2 * HS - 1:HS - 1:-1]),
                   inc="vPr")
                opd(v_, lambda e: e.memset(ap("rep")[:, 64 + HS:64 + HS + 1], 1.0),
                   inc="vPr")
                opd(v_, lambda e: e.tensor_copy(
                    ap("rep")[:, 64 + HS + 1:64 + SUB],
                    ap("scn")[:, 0:HS - 1]), inc="vPr")
                for rep_i in range(NB):
                    opd(v_, lambda e, i=i, rep_i=rep_i: e.tensor_copy(
                        bf("Dqb")[:, i * CH + rep_i * SUB:
                                  i * CH + (rep_i + 1) * SUB],
                        ap("rep")[:, 64:64 + SUB]), inc="vPr")
                import os as _os
                for d in (() if _os.environ.get("SKIP_DQ123") else range(1, 4)):
                    c0 = 32 * d - 17
                    for rep_i in range(NB):
                        opd(v_, lambda e, i=i, d=d, c0=c0, rep_i=rep_i:
                           e.tensor_copy(
                               bf("Dqb")[:, d * CHW + i * CH + rep_i * SUB:
                                         d * CHW + i * CH + (rep_i + 1) * SUB],
                               ap("D1f")[:, i * CH + c0:i * CH + c0 + SUB]),
                           inc="vPr")
                import os as _os2
                if _os2.environ.get("SKIP_TB"):
                    continue
                opd(v_, lambda e, i=i: e.tensor_copy(
                    ap("rep")[:, 0:CH - 1],
                    ap("D1f")[:, i * CH + CH - 2:i * CH - 1 if i > 0
                              else None:-1]), inc="vPr")
                opd(v_, lambda e: e.memset(ap("rep")[:, CH - 1:CH], 1.0),
                   inc="vPr")
                opd(v_, lambda e, i=i: e.tensor_scalar(
                    ap("iotaf")[:, 0:CH], ap("rep")[:, 0:CH],
                    ap("tf")[:, i:i + 1], None, ALU.mult, ALU.bypass),
                   inc="vPr")
                nv = cnt["vPr"]
                op(t_, lambda e: e.matmul(
                    pb[2][:, 0:128], ap("iotaf")[:, 0:CH], ap("identf"),
                    start=True, stop=True, is_transpose=True),
                   waits=[("vPr", nv)], inc="tbA")
                opd(v_, lambda e, i=i: e.tensor_copy(
                    bf("TBf")[:, i * 128:(i + 1) * 128], pb[2][:, 0:128]),
                   waits=[("tbA", cnt["tbA"])], inc="vPr")
            if debug:
                for nm2, buf in [("d_dkb", "Dkb"), ("d_dqb", "Dqb"),
                                 ("d_d1b", "D1b"), ("d_tbf", "TBf")]:
                    op(y_, lambda e, nm2=nm2, buf=buf: e.dma_start(
                        out=dbg[nm2], in_=bf(buf)),
                       waits=[("vPr", cnt["vPr"])], inc="yst", amt=16)
                for nm2, buf in [("d_scn", "scn"), ("d_rep", "rep"),
                                 ("d_zf", "zf")]:
                    op(y_, lambda e, nm2=nm2, buf=buf: e.dma_start(
                        out=dbg[nm2], in_=ap(buf)),
                       waits=[("vPr", cnt["vPr"])], inc="yst", amt=16)
            barrier()
            tbA0 = cnt["tbA"]
            if init_only:
                op(y_, lambda e: e.nop(), waits=[("yst", cnt["yst"])])
                barrier()
                yield "done"
                return

            PCW1 = min(1024, FF)
            PCW2 = min(1024, D)
            dq_pieces = []
            for g in range(NT):
                for pc in range(FF // PCW1):
                    dq_pieces.append((1, g, pc))
            for g in range(FF // 128):
                dq_pieces.append((2, g, 0))
            n_dq = len(dq_pieces)

            def dq_srcs(j):
                which, gidx, pc = dq_pieces[j]
                if which == 1:
                    W_ = PCW1
                    return (W_, q1[gidx * 128:(gidx + 1) * 128,
                                   pc * W_:(pc + 1) * W_],
                            s1r[gidx, :, pc * W_:(pc + 1) * W_],
                            w1_sc[gidx * 128:(gidx + 1) * 128,
                                  pc * W_:(pc + 1) * W_])
                W_ = PCW2
                return (W_, q2[gidx * 128:(gidx + 1) * 128, :],
                        s2r[gidx, :, :],
                        w2_sc[gidx * 128:(gidx + 1) * 128, :])

            def emit_dq_slot(u):
                if u >= 1:
                    W_, _, _, wdst = dq_srcs(u - 1)
                    op(v_, lambda e, W_=W_: e.scalar_tensor_tensor(
                        bf("dq_w")[:, 0:W_], ap("dq_q")[:, 0:W_], -128.0,
                        ap("dq_s")[:, 0:W_], ALU.add, ALU.mult),
                       waits=[("gq", cnt["gq"]), ("yq", cnt["yq"])],
                       inc="vPr")
                    op(y_, lambda e, wdst=wdst, W_=W_: e.dma_start(
                        out=wdst, in_=bf("dq_w")[:, 0:W_]),
                       waits=[("vPr", cnt["vPr"])], inc="yq", amt=16)
                if u < n_dq:
                    W_, qsrc, ssrc, _ = dq_srcs(u)
                    op(g_, lambda e, qsrc=qsrc, W_=W_: e.dma_start(
                        out=ap("dq_q")[:, 0:W_], in_=qsrc),
                       waits=[("vPr", cnt["vPr"])], inc="gq", amt=16)
                    op(g_, lambda e, ssrc=ssrc, W_=W_: e.dma_start(
                        out=ap("dq_s")[:, 0:W_], in_=ssrc), inc="gq", amt=16)

            a_done = []
            for s in range(NS):
                c0 = s * SLAB
                xbu = "xb" if s % 2 == 0 else "xb2"
                op(g_, lambda e, c0=c0, xbu=xbu: e.dma_start(
                    out=fm3(bf(xbu), NT), in_=fm(xT, c0, SLAB)),
                   waits=(a_done[s - 2] if s >= 2 else []), inc="gx", amt=16)
                gx_now = cnt["gx"]

                for i in range(NT):
                    bank = i % 2
                    gt = s * NT + i
                    waits = [("gx", gx_now)]
                    if gt >= 2:
                        waits.append(("vkA", gt - 1))
                    if s > 0 and i < 2:
                        waits.append(("vvA", s * CPS * NVH))
                    for k in range(NT):
                        last = k == NT - 1
                        op(t_, lambda e, i=i, k=k, bank=bank, last=last,
                           xbu=xbu: e.matmul(
                            pb[bank][:, 0:SLAB],
                            bf("wk")[:, k * D + i * 128:k * D + (i + 1) * 128],
                            bf(xbu)[:, k * SLAB:(k + 1) * SLAB],
                            start=(k == 0), stop=last),
                           waits=(waits if k == 0 else ()),
                           inc=("tkA" if last else None))
                    kw = [("tkA", cnt["tkA"])]
                    if s > 0 and i == 0:
                        kw.append(("yst", 16 * 4 * s))
                    op(v_, lambda e, i=i, bank=bank: e.tensor_copy(
                        bf("kf")[:, i * SLAB:(i + 1) * SLAB], pb[bank][:, 0:SLAB]),
                       waits=kw, inc="vkA")
                for i in range(NT):
                    bank = 2 + i % 2
                    gt = s * NT + i
                    waits = [("gx", gx_now)]
                    if gt >= 2:
                        waits.append(("srA", gt - 1))
                    if s > 0 and i < 2:
                        waits.append(("vbA", s * CPS * NT))
                    for k in range(NT):
                        last = k == NT - 1
                        op(t_, lambda e, i=i, k=k, bank=bank, last=last,
                           xbu=xbu: e.matmul(
                            pb[bank][:, 0:SLAB],
                            bf("wr")[:, k * D + i * 128:k * D + (i + 1) * 128],
                            bf(xbu)[:, k * SLAB:(k + 1) * SLAB],
                            start=(k == 0), stop=last),
                           waits=(waits if k == 0 else ()),
                           inc=("trA" if last else None))
                    rw = [("trA", cnt["trA"])]
                    if s > 0 and i == 0:
                        rw.append(("yst", 16 * 4 * s))
                    op(s_, lambda e, i=i, bank=bank: e.activation(
                        bf("rf")[:, i * SLAB:(i + 1) * SLAB], pb[bank][:, 0:SLAB],
                        AF.Sigmoid), waits=rw, inc="srA")
                vk_done = cnt["vkA"]
                for cc in range(CPS):
                    for half in range(NVH):
                        bank = half % 2
                        gtv = (s * CPS + cc) * NVH + half
                        waits = [("gx", gx_now), ("vkA", vk_done)]
                        if gtv >= 2:
                            waits.append(("vvA", gtv - 1))
                        for k in range(NT):
                            last = k == NT - 1
                            op(t_, lambda e, cc=cc, half=half, k=k, bank=bank,
                               last=last, xbu=xbu: e.matmul(
                                   pb[bank][:, 0:VH],
                                   bf(xbu)[:, k * SLAB + cc * CH:
                                           k * SLAB + (cc + 1) * CH],
                                   bf("wv")[:, k * D + half * VH:
                                            k * D + (half + 1) * VH],
                                   start=(k == 0), stop=last),
                               waits=(waits if k == 0 else ()),
                               inc=("tvA" if last else None))
                        vw = [("tvA", cnt["tvA"])]
                        if s > 0 and cc == 0 and half == 0:
                            vw.append(("yst", 16 * 4 * s))
                        op(v_, lambda e, cc=cc, half=half, bank=bank:
                           e.tensor_copy(
                               bf("vt")[:, cc * D + half * VH:
                                        cc * D + (half + 1) * VH],
                               pb[bank][:, 0:VH]),
                           waits=vw, inc="vvA")
                a_done.append([("tkA", cnt["tkA"]), ("trA", cnt["trA"]),
                               ("tvA", cnt["tvA"])])
                kf_done = cnt["vkA"]
                vt_done = cnt["vvA"]
                srA_done = cnt["srA"]
                for cc in range(CPS):
                    kglob = s * CPS + cc
                    cur = "Sf0" if kglob % 2 == 0 else "Sf1"
                    nxt = "Sf1" if kglob % 2 == 0 else "Sf0"
                    for p in range(NT):
                        bank = 2 + p % 2
                        gtb = kglob * NT + p
                        waits = [("vkA", kf_done), ("srA", srA_done)]
                        if gtb >= 2:
                            waits.append(("vbA", gtb - 1))
                        op(t_, lambda e, p=p, cc=cc, bank=bank: e.matmul(
                            pb[bank][:, 0:64].bitcast(BF16),
                            bf("kf")[:, p * SLAB + cc * CH:p * SLAB + (cc + 1) * CH],
                            bf("identb"), start=True, stop=True,
                            is_transpose=True),
                           waits=waits, inc="tbA")
                        bw = [("tbA", cnt["tbA"])]
                        if s > 0 and cc == 0 and p == 0:
                            bw.append(("yst", 16 * 4 * s))
                        op(v_, lambda e, p=p, cc=cc, bank=bank: e.tensor_tensor(
                            bf("Bt")[:, cc * D + p * 128:cc * D + (p + 1) * 128],
                            pb[bank][:, 0:64].bitcast(BF16),
                            bf("TBf")[:, p * 128:(p + 1) * 128], ALU.mult),
                           waits=bw, inc="vbA")
                    for p in range(NT):
                        gtu = kglob * NT + p
                        waits = [("vbA", cnt["vbA"]), ("vvA", vt_done)]
                        if gtu >= 1:
                            waits.append(("vsA", gtu))
                        for P in range(HPT):
                            hcol = p * 128 + P * HD
                            op(t_, lambda e, cc=cc, P=P, hcol=hcol: e.matmul(
                                pb[5][P * HD:(P + 1) * HD, 0:HD],
                                bf("Bt")[:, cc * D + hcol:cc * D + hcol + HD],
                                bf("vt")[:, cc * D + hcol:cc * D + hcol + HD],
                                start=True, stop=True,
                                tile_position=(0, P * HD),
                                skip_group_check=(P > 0)),
                               waits=(waits if P == 0 else ()),
                               inc=("tuA" if P == HPT - 1 else None))
                        op(v_, lambda e, p=p, cur=cur, nxt=nxt:
                           e.scalar_tensor_tensor(
                               ap(nxt)[:, p * HD:(p + 1) * HD],
                               ap(cur)[:, p * HD:(p + 1) * HD],
                               ap("dec128")[:, p:p + 1],
                               pb[5][:, 0:HD], ALU.mult, ALU.add),
                           waits=[("tuA", cnt["tuA"])], inc="vsA")
                op(y_, lambda e, c0=c0: e.dma_start(
                    out=fm(k_sc, c0, SLAB), in_=fm3(bf("kf"), NT)),
                   waits=[("vkA", cnt["vkA"])], inc="yst", amt=16)
                op(y_, lambda e, c0=c0: e.dma_start(
                    out=fm(r_sc, c0, SLAB), in_=fm3(bf("rf"), NT)),
                   waits=[("srA", cnt["srA"])], inc="yst", amt=16)
                op(y_, lambda e, c0=c0: e.dma_start(
                    out=tm(v_sc, c0, SLAB), in_=fm3(bf("vt"), CPS)),
                   waits=[("vvA", cnt["vvA"])], inc="yst", amt=16)
                op(y_, lambda e, c0=c0: e.dma_start(
                    out=tm(b_sc, c0, SLAB), in_=fm3(bf("Bt"), CPS)),
                   waits=[("vbA", cnt["vbA"])], inc="yst", amt=16)
            barrier()

            op(y_, lambda e: e.dma_start(out=ccin, in_=ap("Sf0")),
               inc="yst", amt=16)
            op(g_, lambda e: e.collective_compute(
                "AllGather", ALU.bypass,
                replica_groups=[[0, 1], [2, 3], [4, 5], [6, 7]],
                ins=[ccin.opt()], outs=[ccout.opt()]),
               waits=[("yst", cnt["yst"])], inc="cc")
            op(g_, lambda e: e.dma_start(out=ap("Sin"), in_=ccout[0]),
               waits=[("cc", cnt["cc"])], inc="gsin", amt=16)
            op(v_, lambda e: e.tensor_scalar(
                ap("Sf0"), ap("Sin"), ap("flag")[:, 0:1], None,
                ALU.mult, ALU.bypass),
               waits=[("gsin", cnt["gsin"])], inc="vS1")
            barrier()
            yield "B1"

            op(g_, lambda e: e.dma_start(
                out=fm3(bf("wo"), NT), in_=woT.rearrange("(k p) n -> p k n", p=128)),
               inc="gwo", amt=16)
            wo_ld = cnt["gwo"]
            op(v_, lambda e: e.memset(pb[2][:, 0:512], 0.0), inc="vPr")
            op(v_, lambda e: e.memset(pb[3][:, 0:512], 0.0), inc="vPr")
            op(v_, lambda e: e.memset(pb[4][:, 0:512], 0.0), inc="vPr")
            op(v_, lambda e: e.memset(bf("Sb"), 0.0), inc="vPr")

            for s in range(NS):
                c0 = s * SLAB
                ld_wait = []
                if s > 0:
                    ld_wait = [("tU1", cnt["tU1"]), ("tY", cnt["tY"]),
                               ("vPre", cnt["vPre"]), ("vPr", cnt["vPr"])]
                op(s_, lambda e, c0=c0: e.dma_start(
                    out=fm3(bf("kf"), NT), in_=fm(k_sc, c0, SLAB)),
                   waits=ld_wait + [("yst", 16 * (4 * NS + 1))],
                   inc="sldKR", amt=16)
                op(s_, lambda e, c0=c0: e.dma_start(
                    out=fm3(bf("rf"), NT), in_=fm(r_sc, c0, SLAB)),
                   inc="sldKR", amt=16)
                op(s_, lambda e, c0=c0: e.dma_start(
                    out=fm3(bf("vt"), CPS), in_=tm(v_sc, c0, SLAB)),
                   inc="sldVB", amt=16)
                op(s_, lambda e, c0=c0: e.dma_start(
                    out=fm3(bf("Bt"), CPS), in_=tm(b_sc, c0, SLAB)),
                   inc="sldVB", amt=16)
                op(s_, lambda e, c0=c0: e.dma_start(
                    out=fm3(ap("xf"), NT), in_=fm(xT, c0, SLAB)), inc="sld", amt=16)
                sld_now = cnt["sld"]
                sld_kr = cnt["sldKR"]
                sld_vb = cnt["sldVB"]

                for cc in range(CPS):
                    kglob = s * CPS + cc
                    emit_dq_slot(2 * kglob)
                    cur = "Sf0" if kglob % 2 == 0 else "Sf1"
                    nxt = "Sf1" if kglob % 2 == 0 else "Sf0"
                    pw = [("sldKR", sld_kr), ("vS1", cnt["vS1"])]
                    if kglob > 0:
                        pw += [("tY", cnt["tY"]), ("tA1", cnt["tA1"]),
                               ("tU1", cnt["tU1"])]

                    def rch3():
                        return bf("rf").rearrange("p (i c) -> p i c", i=NT)[
                            :, :, cc * CH:(cc + 1) * CH]

                    def kch3():
                        return bf("kf").rearrange("p (i c) -> p i c", i=NT)[
                            :, :, cc * CH:(cc + 1) * CH]

                    op(v_, lambda e: e.tensor_tensor(
                        bf("qt").rearrange("p (i c) -> p i c", i=NT),
                        rch3(),
                        bf("D1b").rearrange("p (i c) -> p i c", i=NT),
                        ALU.mult), waits=pw, inc="vPr")
                    for d in range(4):
                        op(v_, lambda e, d=d: e.tensor_tensor(
                            bf("Qh")[:, d * CHW:(d + 1) * CHW]
                            .rearrange("p (i c) -> p i c", i=NT),
                            rch3(),
                            bf("Dqb")[:, d * CHW:(d + 1) * CHW]
                            .rearrange("p (i c) -> p i c", i=NT),
                            ALU.mult), inc="vPr")
                    op(v_, lambda e: e.tensor_tensor(
                        bf("kd").rearrange("p (i c) -> p i c", i=NT),
                        kch3(),
                        bf("Dkb").rearrange("p (i c) -> p i c", i=NT),
                        ALU.mult), inc="vPr")
                    op(v_, lambda e, cur=cur: e.tensor_copy(
                        bf("Sb").rearrange("p (i c) -> p i c", i=NT)
                        [0:HD, :, 0:HD],
                        ap(cur).rearrange("p (i c) -> p i c", i=NT)
                        [0:HD, :, :]), inc="vPr")
                    op(v_, lambda e, cur=cur: e.tensor_copy(
                        bf("Sb").rearrange("p (i c) -> p i c", i=NT)
                        [HD:128, :, HD:CH],
                        ap(cur).rearrange("p (i c) -> p i c", i=NT)
                        [HD:128, :, :]), inc="vPr")
                    prep_done = cnt["vPr"]

                    for p in range(NT):
                        for P in range(HPT):
                            bank = 2 + P
                            gA = (kglob * NT + p) * HPT + P
                            wA = [("vPr", prep_done)]
                            if gA >= 2:
                                wA.append(("sAe", gA - 1))
                            tiles = []
                            for J in range(NB):
                                for M in range(J):
                                    tiles.append((M * SUB, SUB, J * SUB, SUB,
                                                  J - M))
                                tiles.append((J * SUB, SUB // 2, J * SUB,
                                              SUB // 2, 0))
                                tiles.append((J * SUB, SUB, J * SUB + SUB // 2,
                                              SUB // 2, 0))
                            for ti, (m0, msz, j0, jsz, dta) in enumerate(tiles):
                                first = ti == 0
                                last = ti == len(tiles) - 1
                                op(t_, lambda e, p=p, P=P, m0=m0, msz=msz,
                                   j0=j0, jsz=jsz, dta=dta, bank=bank,
                                   first=first: e.matmul(
                                       pb[bank][m0:m0 + msz, j0:j0 + jsz],
                                       bf("kd")[P * HD:(P + 1) * HD,
                                                p * CH + m0:p * CH + m0 + msz],
                                       bf("Qh")[P * HD:(P + 1) * HD,
                                                dta * CHW + p * CH + j0:
                                                dta * CHW + p * CH + j0 + jsz],
                                       start=True, stop=True,
                                       tile_position=(P * HD, m0),
                                       skip_group_check=(not first)),
                                   waits=(wA if first else ()),
                                   inc=("tA1" if last else None))
                            op(s_, lambda e, P=P, bank=bank: e.copy(
                                bf("At")[:, (HPT + P) * CH:(HPT + P + 1) * CH],
                                pb[bank][:, 0:CH]),
                               waits=[("tA1", cnt["tA1"]),
                                      ("tY", cnt["tY"])], inc="sAe")
                            op(v_, lambda e, P=P: e.tensor_tensor(
                                bf("At")[:, P * CH:(P + 1) * CH],
                                bf("At")[:, (HPT + P) * CH:(HPT + P + 1) * CH],
                                bf("trimask"), ALU.mult),
                               waits=[("sAe", cnt["sAe"]),
                                      ("tY", cnt["tY"])], inc="vA1")
                        gY = kglob * NT + p
                        wY = [("vPr", prep_done), ("vA1", cnt["vA1"]),
                              ("sldVB", sld_vb)]
                        if gY >= 1:
                            wY.append(("sY", gY))
                        op(t_, lambda e, p=p: e.matmul(
                            pb[4][:, 0:CH], bf("Sb")[:, p * CH:(p + 1) * CH],
                            bf("qt")[:, p * CH:(p + 1) * CH],
                            start=True, stop=True), waits=wY)
                        for P in range(HPT):
                            hcol = p * 128 + P * HD
                            op(t_, lambda e, P=P, hcol=hcol, cc=cc: e.matmul(
                                pb[4][P * HD:(P + 1) * HD, 0:CH],
                                bf("vt")[:, cc * D + hcol:cc * D + hcol + HD],
                                bf("At")[:, P * CH:(P + 1) * CH],
                                start=False, stop=True,
                                tile_position=(0, P * HD),
                                skip_group_check=True),
                               inc=("tY" if P == HPT - 1 else None))
                        op(s_, lambda e, p=p, cc=cc: e.copy(
                            bf("yb")[:, p * SLAB + cc * CH:
                                     p * SLAB + (cc + 1) * CH],
                            pb[4][:, 0:CH]),
                           waits=[("tY", cnt["tY"]), ("tWo", cnt["tWo"])],
                           inc="sY")
                        if debug and s == 0 and cc == 0 and p == NT - 1:
                            op(y_, lambda e: e.dma_start(
                                out=dbg["d_at"], in_=bf("At")),
                               waits=[("vA1", cnt["vA1"])], inc="yst", amt=16)
                            op(y_, lambda e: e.dma_start(
                                out=dbg["d_qh"], in_=bf("Qh")),
                               waits=[("vA1", cnt["vA1"])], inc="yst", amt=16)
                            op(y_, lambda e: e.dma_start(
                                out=dbg["d_kd"], in_=bf("kd")),
                               waits=[("vA1", cnt["vA1"])], inc="yst", amt=16)
                            op(y_, lambda e: e.dma_start(
                                out=dbg["d_qt"], in_=bf("qt")),
                               waits=[("vA1", cnt["vA1"])], inc="yst", amt=16)
                            op(y_, lambda e: e.dma_start(
                                out=dbg["d_sb"], in_=bf("Sb")),
                               waits=[("vA1", cnt["vA1"])], inc="yst", amt=16)
                        gU = kglob * NT + p
                        wU = [("sldVB", sld_vb), ("vS1", cnt["vS1"])]
                        for P in range(HPT):
                            hcol = p * 128 + P * HD
                            op(t_, lambda e, P=P, hcol=hcol, cc=cc: e.matmul(
                                pb[5][P * HD:(P + 1) * HD, 0:HD],
                                bf("Bt")[:, cc * D + hcol:cc * D + hcol + HD],
                                bf("vt")[:, cc * D + hcol:cc * D + hcol + HD],
                                start=True, stop=True,
                                tile_position=(0, P * HD),
                                skip_group_check=(P > 0)),
                               waits=(wU if P == 0 else ()),
                               inc=("tU1" if P == HPT - 1 else None))
                        op(v_, lambda e, p=p, cur=cur, nxt=nxt:
                           e.scalar_tensor_tensor(
                               ap(nxt)[:, p * HD:(p + 1) * HD],
                               ap(cur)[:, p * HD:(p + 1) * HD],
                               ap("dec128")[:, p:p + 1],
                               pb[5][:, 0:HD], ALU.mult, ALU.add),
                           waits=[("tU1", cnt["tU1"])], inc="vS1")
                    emit_dq_slot(2 * kglob + 1)

                sy_done = cnt["sY"]
                for i in range(NT):
                    bank = i % 2
                    gt = s * NT + i
                    waits = [("sY", sy_done), ("sld", sld_now),
                             ("gwo", wo_ld), ("vLN", cnt["vLN"])]
                    if gt >= 2:
                        waits.append(("vPre", gt - 1))
                    for k in range(NT):
                        last = k == NT - 1
                        op(t_, lambda e, i=i, k=k, bank=bank, last=last:
                           e.matmul(
                               pb[bank][:, 0:SLAB],
                               bf("wo")[:, k * D + i * 128:k * D + (i + 1) * 128],
                               bf("yb")[:, k * SLAB:(k + 1) * SLAB],
                               start=(k == 0), stop=last),
                           waits=(waits if k == 0 else ()),
                           inc=("tWo" if last else None))
                    op(v_, lambda e, i=i, bank=bank: e.scalar_tensor_tensor(
                        ap("pre1")[:, i * SLAB:(i + 1) * SLAB],
                        pb[bank][:, 0:SLAB], 1.0,
                        ap("xf")[:, i * SLAB:(i + 1) * SLAB],
                        ALU.mult, ALU.add),
                       waits=[("tWo", cnt["tWo"])], inc="vPre")
                if debug:
                    op(y_, lambda e, c0=c0: e.dma_start(
                        out=fm(dbg["d_pre"], c0, SLAB), in_=fm3(ap("pre1"), NT)),
                       waits=[("vPre", cnt["vPre"])], inc="yst", amt=16)
                    op(y_, lambda e, c0=c0: e.dma_start(
                        out=fm(dbg["d_yb"], c0, SLAB), in_=fm3(bf("yb"), NT)),
                       waits=[("sY", cnt["sY"])], inc="yst", amt=16)
                ln_block("pre1", "x1", None, None, None, EPS,
                         "tE", "vLN", "sSq", [("vPre", cnt["vPre"])],
                         e1b=0, e2b=1)
                op(y_, lambda e, c0=c0: e.dma_start(
                    out=fm(x1_sc, c0, SLAB), in_=fm3(ap("x1"), NT)),
                   waits=[("vLN", cnt["vLN"])], inc="yx1", amt=16)
            for u in range(2 * NS * CPS, n_dq + 1):
                emit_dq_slot(u)
            barrier()
            yield "B2h0"

            for half in range(2):
                sldn = "sld2" if half == 0 else "sld3"
                tf1 = "tF1" if half == 0 else "tG1"
                sf1 = "sF1" if half == 0 else "sG1"
                tf2 = "tF2" if half == 0 else "tG2"
                vf2 = "vF2" if half == 0 else "vG2"
                vlnn = "vLN2" if half == 0 else "vLN3"
                z_incs = 1 if half == 0 else 3
                wwait = [("yq", cnt["yq"]), ("yq2", cnt["yq2"])]
                if half == 1:
                    wwait += [("tF1", cnt["tF1"]), ("tF2", cnt["tF2"])]
                op(s_, lambda e, half=half: e.dma_start(
                    out=fm3(bf("w1h"), NT),
                    in_=w1_sc[:, half * FFH:(half + 1) * FFH]
                    .rearrange("(k p) n -> p k n", p=128)),
                   waits=wwait, inc=sldn, amt=16)
                op(s_, lambda e, half=half: e.dma_start(
                    out=fm3(bf("w2h"), NKH),
                    in_=w2_sc[half * FFH:(half + 1) * FFH, :]
                    .rearrange("(k p) n -> p k n", p=128)),
                   inc=sldn, amt=16)
                snaps = []
                for s2 in range(NS):
                    c0 = s2 * SLAB
                    par = s2 % 2
                    x1u = "x1" if par == 0 else "x1b"
                    z1u = "z1" if par == 0 else "z1b"
                    mu_ = "m0" if par == 0 else "m1"
                    ldw = []
                    if s2 >= 2:
                        sn = snaps[s2 - 2]
                        ldw = [(tf1, sn[0]), (tf2, sn[1]),
                               (vlnn, sn[2]), (vf2, sn[3]),
                               ("sSq2", sn[4])]
                        if half == 0:
                            ldw += [("yz1", sn[5]), ("yh", sn[6])]
                        else:
                            ldw += [("yout", sn[7])]
                    op(s_, lambda e, c0=c0, x1u=x1u: e.dma_start(
                        out=fm3(ap(x1u), NT), in_=fm(x1_sc, c0, SLAB)),
                       waits=ldw + [("yx1", 16 * NS)], inc=sldn, amt=16)
                    if half == 1:
                        hbw = [("yh", cnt["yh"])]
                        if s2 >= 1:
                            hbw.append((tf1, snaps[s2 - 1][0]))
                        op(s_, lambda e, c0=c0: e.dma_start(
                            out=fm3(bf("hb"), NT), in_=fm(h_sc, c0, SLAB)),
                           waits=hbw, inc=sldn, amt=16)
                        op(s_, lambda e, c0=c0, z1u=z1u: e.dma_start(
                            out=fm3(ap(z1u), NT), in_=fm(z1_sc, c0, SLAB)),
                           waits=[("yz1", cnt["yz1"])], inc=sldn, amt=16)
                    sld_now2 = cnt[sldn]

                    if half == 0:
                        lnw = [(sldn, sld_now2)]
                        if s2 >= 1:
                            lnw.append(("yh", snaps[s2 - 1][6]))
                        ln_block(x1u, None, "hb", "gam", "bet", EPS,
                                 "tE", "vLN2", "sSq2", lnw)
                        op(y_, lambda e, c0=c0: e.dma_start(
                            out=fm(h_sc, c0, SLAB), in_=fm3(bf("hb"), NT)),
                           waits=[("vLN2", cnt["vLN2"])], inc="yh", amt=16)
                        hb_ready = [("vLN2", cnt["vLN2"])]
                    else:
                        hb_ready = [(sldn, sld_now2)]
                    for oh in range(NOH):
                        bank = oh % 2
                        gt = s2 * NOH + oh
                        waits = list(hb_ready) + [(sldn, sld_now2)]
                        if gt >= 2:
                            waits.append((sf1, gt - 1))
                        for k in range(NT):
                            last = k == NT - 1
                            op(t_, lambda e, oh=oh, k=k, bank=bank, last=last:
                               e.matmul(
                                   pb[bank][:, 0:SLAB],
                                   bf("w1h")[:, k * FFH + oh * 128:
                                             k * FFH + (oh + 1) * 128],
                                   bf("hb")[:, k * SLAB:(k + 1) * SLAB],
                                   start=(k == 0), stop=last),
                               waits=(waits if k == 0 else ()),
                               inc=(tf1 if last else None))
                        evw = [(tf1, cnt[tf1])]
                        if half == 1 and oh == 0 and s2 >= 1:
                            evw.append(("yout", snaps[s2 - 1][7]))
                        op(s_, lambda e, oh=oh, bank=bank, half=half:
                           e.activation(
                               bf("ab")[:, oh * SLAB:(oh + 1) * SLAB],
                               pb[bank][:, 0:SLAB], AF.Silu,
                               bias=ap("b1")[:, half * NOH + oh:
                                             half * NOH + oh + 1]),
                           waits=evw, inc=sf1)
                    for i in range(NT):
                        bank = 2 + i % 2
                        gt = s2 * NT + i
                        waits = [(sf1, cnt[sf1])]
                        if gt >= 2:
                            waits.append((vf2, z_incs * (gt - 1) + 1))
                        for k in range(NKH):
                            last = k == NKH - 1
                            op(t_, lambda e, i=i, k=k, bank=bank, last=last:
                               e.matmul(
                                   pb[bank][:, 0:SLAB],
                                   bf("w2h")[:, k * D + i * 128:
                                             k * D + (i + 1) * 128],
                                   bf("ab")[:, k * SLAB:(k + 1) * SLAB],
                                   start=(k == 0), stop=last),
                               waits=(waits if k == 0 else ()),
                               inc=(tf2 if last else None))
                        if half == 0:
                            op(v_, lambda e, i=i, bank=bank, z1u=z1u:
                               e.tensor_copy(
                                   ap(z1u)[:, i * SLAB:(i + 1) * SLAB],
                                   pb[bank][:, 0:SLAB]),
                               waits=[(tf2, cnt[tf2])], inc=vf2)
                        else:
                            op(v_, lambda e, i=i, bank=bank, z1u=z1u:
                               e.tensor_tensor(
                                   ap("tz"),
                                   ap(z1u)[:, i * SLAB:(i + 1) * SLAB],
                                   pb[bank][:, 0:SLAB], ALU.add),
                               waits=[(tf2, cnt[tf2])], inc=vf2)
                            op(v_, lambda e, i=i: e.tensor_scalar(
                                ap("tz2"), ap("tz"), ap("b2")[:, i:i + 1],
                                0.25, ALU.add, ALU.mult), inc=vf2)
                            op(v_, lambda e, i=i, mu_=mu_, x1u=x1u:
                               e.tensor_tensor(
                                   ap(mu_)[:, i * SLAB:(i + 1) * SLAB],
                                   ap("tz2"),
                                   ap(x1u)[:, i * SLAB:(i + 1) * SLAB],
                                   ALU.add),
                               inc=vf2)
                    if half == 0:
                        op(y_, lambda e, c0=c0, z1u=z1u: e.dma_start(
                            out=fm(z1_sc, c0, SLAB), in_=fm3(ap(z1u), NT)),
                           waits=[("vF2", cnt["vF2"])], inc="yz1", amt=16)
                    else:
                        ln_block(mu_, "outb", None, None, None, EPS / 4.0,
                                 "tE", "vLN3", "sSq2", [("vG2", cnt["vG2"])])
                        op(y_, lambda e, c0=c0: e.dma_start(
                            out=fm(outT, c0, SLAB), in_=fm3(ap("outb"), NT)),
                           waits=[("vLN3", cnt["vLN3"])], inc="yout", amt=16)
                    snaps.append((cnt[tf1], cnt[tf2], cnt[vlnn], cnt[vf2],
                                  cnt["sSq2"], cnt["yz1"], cnt["yh"],
                                  cnt["yout"]))
                if half == 0:
                    barrier()
                    yield "B2h1"

            if debug:
                for nm, srcd in [("d_k", k_sc), ("d_r", r_sc), ("d_v", v_sc),
                                 ("d_b", b_sc), ("d_x1", x1_sc), ("d_h", h_sc),
                                 ("d_z1", z1_sc), ("d_cc", ccin),
                                 ("d_w1", w1_sc), ("d_w2", w2_sc)]:
                    op(y_, lambda e, nm=nm, srcd=srcd: e.dma_start(
                        out=dbg[nm], in_=srcd),
                       waits=[("yout", cnt["yout"]), ("yq", cnt["yq"]),
                              ("yq2", cnt["yq2"]),
                              ("yst", cnt["yst"]), ("yx1", cnt["yx1"]),
                              ("yh", cnt["yh"]), ("yz1", cnt["yz1"])],
                       inc="yout", amt=16)
            op(y_, lambda e: e.nop(),
               waits=[("yout", cnt["yout"]), ("yst", cnt["yst"]),
                      ("yq", cnt["yq"]), ("yq2", cnt["yq2"]),
                      ("yx1", cnt["yx1"]),
                      ("yh", cnt["yh"]), ("yz1", cnt["yz1"])])
            barrier()
            yield "done"

        block = E(nc.Block())

        def run_walk(name, handle):
            for tag in walk(name, handle):
                if tag == "done":
                    break

        @block.gpsimd
        def _(g):
            run_walk("gpsimd", g)

        @block.tensor
        def _(t):
            run_walk("tensor", t)

        @block.vector
        def _(v):
            run_walk("vector", v)

        @block.scalar
        def _(s):
            run_walk("scalar", s)

        @block.sync
        def _(y):
            run_walk("sync", y)

    return nc



def _stage(arr, cfg):
    def colmajor(v, nt):
        return np.ascontiguousarray(np.asarray(v, np.float32).reshape(nt, 128).T)
    out = {
        "wkT": np.ascontiguousarray(np.asarray(arr["Wk"], np.float32).T),
        "wvT": np.ascontiguousarray(np.asarray(arr["Wv"], np.float32).T),
        "wrT": np.ascontiguousarray(np.asarray(arr["Wr"], np.float32).T),
        "woT": np.ascontiguousarray(np.asarray(arr["Wo"], np.float32).T),
        "decf": colmajor(np.asarray(arr["decay"]).reshape(-1), cfg.NT),
        "tff": colmajor(np.asarray(arr["tf"]).reshape(-1), cfg.NT),
        "gamf": colmajor(arr["gam"], cfg.NT),
        "betf": colmajor(arr["bet"], cfg.NT),
        "b1f": colmajor(arr["b1"], cfg.FF // 128),
        "b2f": colmajor(arr["b2"], cfg.NT),
        "q1": np.ascontiguousarray(np.asarray(arr["q1"], np.int32)),
        "q2": np.ascontiguousarray(np.asarray(arr["q2"], np.int32)),
        "s1r": np.ascontiguousarray(np.broadcast_to(
            np.asarray(arr["s1"], np.float32)[:, None, :],
            (cfg.D // 128, 128, cfg.FF))),
        "s2r": np.ascontiguousarray(np.broadcast_to(
            np.asarray(arr["s2"], np.float32)[:, None, :],
            (cfg.FF // 128, 128, cfg.D))),
    }
    return out


def kernel(L0_attn_x, L0_attn_Wk, L0_attn_Wv, L0_attn_Wr, L0_attn_Wo,
           L0_attn_time_decay, L0_attn_time_first,
           L0_ffn_ln_gamma, L0_ffn_ln_beta,
           L0_ffn_fc1_q, L0_ffn_fc1_scales, L0_ffn_fc1_bias,
           L0_ffn_fc2_q, L0_ffn_fc2_scales, L0_ffn_fc2_bias):
    cfg = Cfg()
    x = np.asarray(L0_attn_x, np.float32)
    B, L, D = x.shape
    TOK = cfg.TOK
    nc = build_program(cfg)
    common = _stage({
        "Wk": L0_attn_Wk, "Wv": L0_attn_Wv, "Wr": L0_attn_Wr, "Wo": L0_attn_Wo,
        "decay": L0_attn_time_decay, "tf": L0_attn_time_first,
        "gam": L0_ffn_ln_gamma, "bet": L0_ffn_ln_beta,
        "q1": L0_ffn_fc1_q, "s1": L0_ffn_fc1_scales, "b1": L0_ffn_fc1_bias,
        "q2": L0_ffn_fc2_q, "s2": L0_ffn_fc2_scales, "b2": L0_ffn_fc2_bias,
    }, cfg)

    in_maps = []
    for core in range(8):
        b, h = core // 2, core % 2
        m = dict(common)
        m["xT"] = np.ascontiguousarray(x[b, h * TOK:(h + 1) * TOK, :].T)
        m["flagf"] = np.full((128, 1), float(h), np.float32)
        in_maps.append(m)

    res = run_bass_kernel_spmd(nc, in_maps, core_ids=list(range(8)))

    out = np.empty((B, L, D), np.float32)
    for core in range(8):
        b, h = core // 2, core % 2
        out[b, h * TOK:(h + 1) * TOK, :] = np.asarray(
            res.results[core]["outT"], np.float32).T
    return out


# version 14
# speedup vs baseline: 1.0286x; 1.0286x over previous
import numpy as np
import concourse.bass as bass
import concourse.mybir as mybir
from concourse.bass_utils import run_bass_kernel_spmd

F32 = mybir.dt.float32
BF16 = mybir.dt.bfloat16
I32 = mybir.dt.int32
AF = mybir.ActivationFunctionType
ALU = mybir.AluOpType

EPS = 1e-5
MAGIC = 0x5F3759E0


class Cfg:
    def __init__(self, D=1024, FF=4096, TOK=4096, SLAB=512, HD=64):
        self.D, self.FF, self.TOK, self.SLAB, self.HD = D, FF, TOK, SLAB, HD
        self.CH = 128
        self.SUB = 32
        self.NT = D // 128
        self.NS = TOK // SLAB
        self.CPS = SLAB // self.CH
        self.FFH = FF // 2


class Arena:
    def __init__(self, tensor, width):
        self.t, self.width = tensor, width
        self.free = [(0, width)]
        self.live = {}
        self.layout = {}
        self.hwm = 0

    ALIGN = 16

    def alloc(self, name, ncols):
        assert name not in self.live, name
        A = self.ALIGN
        cands = []
        for idx, (off, sz) in enumerate(self.free):
            aoff = (off + A - 1) // A * A
            pad = aoff - off
            if sz >= ncols + pad:
                cands.append((sz, idx, aoff, pad))
        cands.sort()
        for _, idx, aoff, pad in cands[:1]:
            off, sz = self.free[idx]
            self.free[idx] = (aoff + ncols, sz - pad - ncols)
            if self.free[idx][1] == 0:
                self.free.pop(idx)
            if pad:
                self.free.append((off, pad))
                self.free.sort()
            self.live[name] = (aoff, ncols)
            self.layout[name] = (aoff, ncols)
            self.hwm = max(self.hwm, aoff + ncols)
            return
        raise RuntimeError(f"arena OOM: {name} ({ncols}); free={self.free}")

    def release(self, name):
        off, sz = self.live.pop(name)
        self.layout.setdefault(name, (off, sz))
        self.free.append((off, sz))
        self.free.sort()
        merged = []
        for off, sz in self.free:
            if merged and merged[-1][0] + merged[-1][1] == off:
                merged[-1][1] += sz
            else:
                merged.append([off, sz])
        self.free = [(a, b) for a, b in merged]


def build_program(cfg: Cfg, arena_cols=52200, debug=False, init_only=False):
    c = cfg
    D, FF, TOK, SLAB, CH, SUB = c.D, c.FF, c.TOK, c.SLAB, c.CH, c.SUB
    NT, NS, CPS, HD, FFH = c.NT, c.NS, c.CPS, c.HD, c.FFH
    HPT = 128 // HD
    CHW = NT * CH
    NOH = FFH // 128
    NKH = FFH // 128
    NB = CH // SUB
    VH = min(512, D)
    NVH = D // VH

    nc = bass.Bass("TRN2", target_bir_lowering=False, detect_race_conditions=False)

    xT = nc.dram_tensor("xT", [D, TOK], F32, kind="ExternalInput").ap()
    wkT = nc.dram_tensor("wkT", [D, D], F32, kind="ExternalInput").ap()
    wvT = nc.dram_tensor("wvT", [D, D], F32, kind="ExternalInput").ap()
    wrT = nc.dram_tensor("wrT", [D, D], F32, kind="ExternalInput").ap()
    woT = nc.dram_tensor("woT", [D, D], F32, kind="ExternalInput").ap()
    decf = nc.dram_tensor("decf", [128, NT], F32, kind="ExternalInput").ap()
    tff = nc.dram_tensor("tff", [128, NT], F32, kind="ExternalInput").ap()
    gamf = nc.dram_tensor("gamf", [128, NT], F32, kind="ExternalInput").ap()
    betf = nc.dram_tensor("betf", [128, NT], F32, kind="ExternalInput").ap()
    b1f = nc.dram_tensor("b1f", [128, FF // 128], F32, kind="ExternalInput").ap()
    b2f = nc.dram_tensor("b2f", [128, NT], F32, kind="ExternalInput").ap()
    q1 = nc.dram_tensor("q1", [D, FF], I32, kind="ExternalInput").ap()
    q2 = nc.dram_tensor("q2", [FF, D], I32, kind="ExternalInput").ap()
    s1r = nc.dram_tensor("s1r", [D // 128, 128, FF], F32, kind="ExternalInput").ap()
    s2r = nc.dram_tensor("s2r", [FF // 128, 128, D], F32, kind="ExternalInput").ap()
    flagf = nc.dram_tensor("flagf", [128, 1], F32, kind="ExternalInput").ap()
    outT = nc.dram_tensor("outT", [D, TOK], F32, kind="ExternalOutput").ap()

    k_sc = nc.dram_tensor("k_sc", [D, TOK], BF16).ap()
    r_sc = nc.dram_tensor("r_sc", [D, TOK], BF16).ap()
    v_sc = nc.dram_tensor("v_sc", [TOK, D], BF16).ap()
    b_sc = nc.dram_tensor("b_sc", [TOK, D], BF16).ap()
    w1_sc = nc.dram_tensor("w1_sc", [D, FF], BF16).ap()
    w2_sc = nc.dram_tensor("w2_sc", [FF, D], BF16).ap()
    x1_sc = nc.dram_tensor("x1_sc", [D, TOK], F32).ap()
    h_sc = nc.dram_tensor("h_sc", [D, TOK], BF16).ap()
    z1_sc = nc.dram_tensor("z1_sc", [D, TOK], F32).ap()
    ccin = nc.dram_tensor("ccin", [128, NT * HD], F32).ap()
    ccout = nc.dram_tensor("ccout", [2, 128, NT * HD], F32).ap()
    dbg = {}
    if debug:
        for nm, shape, dt in [("d_k", [D, TOK], BF16), ("d_r", [D, TOK], BF16),
                              ("d_v", [TOK, D], BF16), ("d_b", [TOK, D], BF16),
                              ("d_x1", [D, TOK], F32), ("d_h", [D, TOK], BF16),
                              ("d_z1", [D, TOK], F32), ("d_cc", [128, NT * HD], F32),
                              ("d_w1", [D, FF], BF16), ("d_w2", [FF, D], BF16),
                              ("d_pre", [D, TOK], F32), ("d_yb", [D, TOK], BF16),
                              ("d_at", [128, 4 * 128], BF16),
                              ("d_scn", [128, 128], F32),
                              ("d_rep", [128, 128], F32),
                              ("d_zf", [128, 128], F32),
                              ("d_dkb", [128, NT * CH], BF16),
                              ("d_dqb", [128, 4 * NT * CH], BF16),
                              ("d_d1b", [128, NT * CH], BF16),
                              ("d_tbf", [128, NT * 128], BF16),
                              ("d_qh", [128, 4 * 1024], BF16),
                              ("d_kd", [128, 1024], BF16),
                              ("d_qt", [128, 1024], BF16),
                              ("d_sb", [128, 1024], BF16)]:
            dbg[nm] = nc.dram_tensor(nm, shape, dt, kind="ExternalOutput").ap()

    def fm(dram_ap, col0, ncols):
        return dram_ap[:, col0:col0 + ncols].rearrange("(i p) c -> p i c", p=128)

    def tm(dram_ap, row0, nrows):
        return dram_ap[row0:row0 + nrows, :].rearrange("(cc t) n -> t cc n", t=128)

    sem_names = [
        "gx", "gw", "gq", "yq", "yq2", "gcs", "gsin", "gwo",
        "tkA", "vkA", "trA", "srA", "tvA", "vvA", "tbA", "vbA", "tuA", "vsA",
        "yst", "cc", "vPr",
        "sld", "tA1", "sAe", "vA1", "tY", "sY", "tU1", "vS1",
        "tWo", "vPre", "tE", "vLN", "sSq", "yx1",
        "sld2", "tF1", "sF1", "tF2", "vF2", "vLN2", "sSq2", "yz1", "yh",
        "sld3", "tG1", "sG1", "tG2", "vG2", "vLN3", "yout", "ph",
    ]

    from contextlib import ExitStack
    with ExitStack() as ctx:
        E = ctx.enter_context
        ar_t = E(nc.sbuf_tensor("arena", [128, arena_cols], F32))
        ar = Arena(ar_t, arena_cols)
        sems = {n: E(nc.semaphore(n)) for n in sem_names}
        pb = [E(nc.psum_tensor(f"pb{i}", [128, 512], F32)) for i in range(6)]

        A = {}

        def al(name, cols):
            ar.alloc(name, cols)

        def ap(name):
            off, sz = ar.layout[name]
            return ar_t[:, off:off + sz]

        def bf(name):
            return ap(name).bitcast(BF16)

        def iv(name):
            return ap(name).bitcast(I32)

        def fm3(apx, n):
            return apx.rearrange("p (i c) -> p i c", i=n)

        for nm, ncol in [("dec", NT), ("tf", NT), ("gam", NT), ("bet", NT),
                         ("b2", NT), ("b1", FF // 128), ("flag", 1),
                         ("dec128", NT), ("rdec", NT), ("sA", 2 * NT),
                         ("zf", 128), ("scn", 128), ("rep", 128),
                         ("iotaf", 128),
                         ("D1f", CHW), ("D1b", CHW // 2), ("Dkb", CHW // 2),
                         ("Dqb", 4 * CHW // 2), ("TBf", NT * 128 // 2),
                         ("trimask", CH // 2), ("identb", 64), ("identf", 128),
                         ("ones", 64),
                         ("Sf0", NT * HD), ("Sf1", NT * HD), ("Sin", NT * HD)]:
            al(nm, ncol)
        for nm, ncol in [("wk", NT * D // 2), ("wv", NT * D // 2),
                         ("wr", NT * D // 2), ("xb", NT * SLAB // 2),
                         ("xb2", NT * SLAB // 2),
                         ("kf", NT * SLAB // 2), ("rf", NT * SLAB // 2),
                         ("vt", CPS * D // 2), ("Bt", CPS * D // 2),
                         ("dq_q", 1024), ("dq_s", 1024), ("dq_w", 512)]:
            al(nm, ncol)

        def restructure(tag):
            if tag == "B1":
                for nm in ["wk", "wv", "wr", "xb", "xb2"]:
                    ar.release(nm)
                for nm, ncol in [("x1", NT * SLAB),
                                 ("pbm", NT * SLAB // 2), ("sq", NT * SLAB // 2),
                                 ("mu", SLAB), ("istd", SLAB), ("t_a", SLAB),
                                 ("t_b", SLAB), ("t_c", SLAB),
                                 ("wo", NT * D // 2), ("xf", NT * SLAB),
                                 ("yb", NT * SLAB // 2), ("pre1", NT * SLAB),
                                 ("qt", CHW // 2),
                                 ("Qh", 4 * CHW // 2), ("kd", CHW // 2),
                                 ("At", HPT * CH), ("At2", HPT * CH),
                                 ("Sb", CHW // 2)]:
                    al(nm, ncol)
            elif tag == "B2h0":
                for nm in ["wo", "kf", "rf", "vt", "Bt", "yb", "pre1",
                           "qt", "Qh", "kd", "At", "At2", "Sb", "xf",
                           "dq_q", "dq_s", "dq_w",
                           "D1f", "D1b", "Dkb", "Dqb", "TBf", "trimask",
                           "identb", "identf", "Sf0", "Sf1", "Sin",
                           "scn", "rep", "zf", "iotaf", "sA"]:
                    ar.release(nm)
                items = [("w1h", NT * FFH // 2), ("w2h", NKH * D // 2),
                         ("hb", NT * SLAB // 2), ("ab", NOH * SLAB // 2),
                         ("z1", NT * SLAB), ("z1b", NT * SLAB),
                         ("x1b", NT * SLAB), ("tz", SLAB), ("tz2", SLAB)]
                for nm, ncol in sorted(items, key=lambda t: -t[1]):
                    al(nm, ncol)
                ar.layout["m0"] = ar.layout["z1"]
                ar.layout["m1"] = ar.layout["z1b"]
                if ar.layout["ab"][1] >= NT * SLAB:
                    ar.layout["outb"] = (ar.layout["ab"][0], NT * SLAB)
                else:
                    al("outb", NT * SLAB)
            elif tag == "B2h1":
                pass

        restructure("B1")
        restructure("B2h0")
        restructure("B2h1")

        def walk(me, eng):
            cnt = {n: 0 for n in sem_names}

            def op(name, fn, waits=(), inc=None, amt=1, drain=False):
                if name == me:
                    for sname, val in waits:
                        if val > 0:
                            eng.wait_ge(sems[sname], val)
                    instr = fn(eng)
                    if inc is not None:
                        instr.then_inc(sems[inc], amt)
                    if drain:
                        eng.drain()
                if inc is not None:
                    cnt[inc] += amt
                return cnt[inc] if inc else None

            def opd(name, fn, waits=(), inc=None, amt=1):
                return op(name, fn, waits=waits, inc=inc, amt=amt, drain=True)

            def barrier():
                target = cnt["ph"] + 5
                for en in ("gpsimd", "tensor", "vector", "scalar", "sync"):
                    if en == me:
                        eng.drain()
                        eng.nop().then_inc(sems["ph"], 1)
                    cnt["ph"] += 1
                eng.wait_ge(sems["ph"], target)

            g_, t_, v_, s_, y_ = "gpsimd", "tensor", "vector", "scalar", "sync"

            def ln_block(src, dst, dst_bf, gamma, beta, eps, tEs, vLNs, sSqs,
                         src_waits, e1b=2, e2b=3):
                invD = 1.0 / D
                w0 = list(src_waits)
                if cnt[tEs] > 0:
                    w0.append((tEs, cnt[tEs]))
                for i in range(NT):
                    op(v_, lambda e, i=i: e.tensor_copy(
                        bf("pbm")[:, i * SLAB:(i + 1) * SLAB],
                        ap(src)[:, i * SLAB:(i + 1) * SLAB]),
                       waits=(w0 if i == 0 else ()), inc=vLNs)
                    op(s_, lambda e, i=i: e.activation(
                        bf("sq")[:, i * SLAB:(i + 1) * SLAB],
                        ap(src)[:, i * SLAB:(i + 1) * SLAB], AF.Square),
                       waits=(w0 if i == 0 else ()), inc=sSqs)
                mir_done, sq_done = cnt[vLNs], cnt[sSqs]
                for srcbuf, bank, wsem, wval in (("pbm", e1b, vLNs, mir_done),
                                                 ("sq", e2b, sSqs, sq_done)):
                    for k in range(NT):
                        last = k == NT - 1
                        op(t_, lambda e, k=k, bank=bank, srcbuf=srcbuf,
                           last=last: e.matmul(
                               pb[bank][:, 0:SLAB], bf("ones"),
                               bf(srcbuf)[:, k * SLAB:(k + 1) * SLAB],
                               start=(k == 0), stop=last),
                           waits=([(wsem, wval)] if k == 0 else ()),
                           inc=(tEs if last else None))
                op(v_, lambda e: e.tensor_scalar(
                    ap("mu"), pb[e1b][:, 0:SLAB], invD, None, ALU.mult, ALU.bypass),
                   waits=[(tEs, cnt[tEs])], inc=vLNs)
                op(v_, lambda e, eps=eps: e.tensor_scalar(
                    ap("t_a"), pb[e2b][:, 0:SLAB], invD, eps, ALU.mult, ALU.add),
                   inc=vLNs)
                op(v_, lambda e: e.scalar_tensor_tensor(
                    ap("t_b"), ap("mu"), -1.0, ap("mu"), ALU.mult, ALU.mult),
                   inc=vLNs)
                op(v_, lambda e: e.tensor_tensor(
                    ap("t_c"), ap("t_a"), ap("t_b"), ALU.add), inc=vLNs)
                op(v_, lambda e: e.tensor_scalar(
                    iv("t_a"), iv("t_c"), 1, -1,
                    ALU.arith_shift_right, ALU.bitwise_xor), inc=vLNs)
                op(v_, lambda e: e.tensor_scalar(
                    iv("t_b"), iv("t_a"), MAGIC, None, ALU.add, ALU.bypass),
                   inc=vLNs)
                curb, nxtb = "t_b", "istd"
                for _ in range(3):
                    op(v_, lambda e, curb=curb: e.tensor_tensor(
                        ap("t_a"), ap(curb), ap(curb), ALU.mult), inc=vLNs)
                    op(v_, lambda e: e.scalar_tensor_tensor(
                        ap("t_a"), ap("t_a"), -0.5, ap("t_c"),
                        ALU.mult, ALU.mult), inc=vLNs)
                    op(v_, lambda e: e.tensor_scalar(
                        ap("t_a"), ap("t_a"), 1.5, None, ALU.add, ALU.bypass),
                       inc=vLNs)
                    op(v_, lambda e, curb=curb, nxtb=nxtb: e.tensor_tensor(
                        ap(nxtb), ap(curb), ap("t_a"), ALU.mult), inc=vLNs)
                    curb, nxtb = nxtb, curb
                assert curb == "istd"
                for i in range(NT):
                    op(v_, lambda e, i=i: e.tensor_tensor(
                        ap("t_b"), ap(src)[:, i * SLAB:(i + 1) * SLAB],
                        ap("mu"), ALU.subtract), inc=vLNs)
                    if dst is not None:
                        op(v_, lambda e, i=i: e.tensor_tensor(
                            ap(dst)[:, i * SLAB:(i + 1) * SLAB], ap("t_b"),
                            ap("istd"), ALU.mult), inc=vLNs)
                    else:
                        op(v_, lambda e: e.tensor_tensor(
                            ap("t_a"), ap("t_b"), ap("istd"), ALU.mult),
                           inc=vLNs)
                        op(v_, lambda e, i=i: e.tensor_scalar(
                            bf(dst_bf)[:, i * SLAB:(i + 1) * SLAB], ap("t_a"),
                            ap(gamma)[:, i:i + 1], ap(beta)[:, i:i + 1],
                            ALU.mult, ALU.add), inc=vLNs)

            for nm, src in [("dec", decf), ("tf", tff), ("gam", gamf),
                            ("bet", betf), ("b2", b2f), ("b1", b1f),
                            ("flag", flagf)]:
                op(g_, lambda e, nm=nm, src=src: e.dma_start(out=ap(nm), in_=src),
                   inc="gw", amt=16)
            for nm, src in [("wk", wkT), ("wv", wvT), ("wr", wrT)]:
                op(g_, lambda e, nm=nm, src=src: e.dma_start(
                    out=fm3(bf(nm), NT), in_=src.rearrange("(k p) n -> p k n", p=128)),
                   inc="gw", amt=16)
            gw_all = cnt["gw"]
            op(g_, lambda e: e.iota(ap("iotaf"), pattern=[[1, 128]], base=0,
                                    channel_multiplier=-1,
                                    allow_small_or_imprecise_dtypes=True),
               inc="gcs")
            gi = cnt["gcs"]

            IW = [("gw", gw_all), ("gcs", gi)]
            opd(v_, lambda e: e.memset(ap("zf"), 0.0), waits=IW, inc="vPr")
            opd(v_, lambda e: e.memset(bf("ones"), 1.0), inc="vPr")
            opd(v_, lambda e: e.memset(ap("Sf0"), 0.0), inc="vPr")
            opd(v_, lambda e: e.memset(ap("Sf1"), 0.0), inc="vPr")
            opd(v_, lambda e: e.memset(pb[5][:, 0:512], 0.0), inc="vPr")
            opd(v_, lambda e: e.tensor_scalar(
                bf("trimask"), ap("iotaf"), 0.0, None, ALU.is_ge, ALU.bypass),
               inc="vPr")
            opd(v_, lambda e: e.tensor_scalar(
                bf("identb"), ap("iotaf"), 0.0, None, ALU.is_equal, ALU.bypass),
               inc="vPr")
            opd(v_, lambda e: e.tensor_scalar(
                ap("identf"), ap("iotaf"), 0.0, None, ALU.is_equal, ALU.bypass),
               inc="vPr")
            for i in range(NT):
                opd(v_, lambda e, i=i: e.tensor_scalar(
                    ap("rep"), ap("zf"), ap("dec")[:, i:i + 1], None,
                    ALU.add, ALU.bypass), inc="vPr")
                opd(v_, lambda e, i=i: e.tensor_tensor_scan(
                    ap("D1f")[:, i * CH:(i + 1) * CH], ap("rep")[:, 0:CH],
                    ap("zf")[:, 0:CH], 1.0, ALU.mult, ALU.add), inc="vPr")
            opd(v_, lambda e: e.tensor_copy(bf("D1b"), ap("D1f")), inc="vPr")
            for i in range(NT):
                opd(v_, lambda e, i=i: e.tensor_copy(
                    ap("dec128")[:, i:i + 1],
                    ap("D1f")[:, i * CH + CH - 1:i * CH + CH]), inc="vPr")
            opd(v_, lambda e: e.reciprocal(ap("rdec"), ap("dec")), inc="vPr")
            opd(v_, lambda e: e.reciprocal(
                ap("sA")[:, 0:NT],
                ap("D1f").rearrange("p (i c) -> p i c", i=NT)[:, :, 16:17]
                .rearrange("p i c -> p (i c)")), inc="vPr")
            HS = SUB // 2
            for i in range(NT):
                opd(v_, lambda e, i=i: e.tensor_scalar(
                    ap("rep"), ap("zf"), ap("dec")[:, i:i + 1], None,
                    ALU.add, ALU.bypass), inc="vPr")
                opd(v_, lambda e: e.tensor_tensor_scan(
                    ap("scn")[:, 0:HS], ap("rep")[:, 0:HS],
                    ap("zf")[:, 0:HS], 1.0, ALU.mult, ALU.add), inc="vPr")
                opd(v_, lambda e, i=i: e.tensor_scalar(
                    ap("rep"), ap("zf"), ap("rdec")[:, i:i + 1], None,
                    ALU.add, ALU.bypass), inc="vPr")
                opd(v_, lambda e: e.tensor_tensor_scan(
                    ap("scn")[:, HS:2 * HS], ap("rep")[:, 0:HS],
                    ap("zf")[:, 0:HS], 1.0, ALU.mult, ALU.add), inc="vPr")
                opd(v_, lambda e: e.tensor_copy(
                    ap("rep")[:, 0:HS], ap("scn")[:, HS - 1::-1]), inc="vPr")
                opd(v_, lambda e: e.memset(ap("rep")[:, HS:HS + 1], 1.0),
                   inc="vPr")
                opd(v_, lambda e: e.tensor_copy(
                    ap("rep")[:, HS + 1:SUB],
                    ap("scn")[:, HS:2 * HS - 1]), inc="vPr")
                opd(v_, lambda e, i=i: e.tensor_scalar(
                    ap("scn")[:, 64:64 + SUB], ap("rep")[:, 0:SUB],
                    ap("tf")[:, i:i + 1], None, ALU.mult, ALU.bypass),
                   inc="vPr")
                for rep_i in range(NB):
                    opd(v_, lambda e, i=i, rep_i=rep_i: e.tensor_copy(
                        bf("Dkb")[:, i * CH + rep_i * SUB:
                                  i * CH + (rep_i + 1) * SUB],
                        ap("scn")[:, 64:64 + SUB]), inc="vPr")
                opd(v_, lambda e: e.tensor_copy(
                    ap("rep")[:, 64:64 + HS], ap("scn")[:, 2 * HS - 1:HS - 1:-1]),
                   inc="vPr")
                opd(v_, lambda e: e.memset(ap("rep")[:, 64 + HS:64 + HS + 1], 1.0),
                   inc="vPr")
                opd(v_, lambda e: e.tensor_copy(
                    ap("rep")[:, 64 + HS + 1:64 + SUB],
                    ap("scn")[:, 0:HS - 1]), inc="vPr")
                for rep_i in range(NB):
                    opd(v_, lambda e, i=i, rep_i=rep_i: e.tensor_copy(
                        bf("Dqb")[:, i * CH + rep_i * SUB:
                                  i * CH + (rep_i + 1) * SUB],
                        ap("rep")[:, 64:64 + SUB]), inc="vPr")
                import os as _os
                for d in (() if _os.environ.get("SKIP_DQ123") else range(1, 4)):
                    c0 = 32 * d - 17
                    for rep_i in range(NB):
                        opd(v_, lambda e, i=i, d=d, c0=c0, rep_i=rep_i:
                           e.tensor_copy(
                               bf("Dqb")[:, d * CHW + i * CH + rep_i * SUB:
                                         d * CHW + i * CH + (rep_i + 1) * SUB],
                               ap("D1f")[:, i * CH + c0:i * CH + c0 + SUB]),
                           inc="vPr")
                import os as _os2
                if _os2.environ.get("SKIP_TB"):
                    continue
                opd(v_, lambda e, i=i: e.tensor_copy(
                    ap("rep")[:, 0:CH - 1],
                    ap("D1f")[:, i * CH + CH - 2:i * CH - 1 if i > 0
                              else None:-1]), inc="vPr")
                opd(v_, lambda e: e.memset(ap("rep")[:, CH - 1:CH], 1.0),
                   inc="vPr")
                opd(v_, lambda e, i=i: e.tensor_scalar(
                    ap("iotaf")[:, 0:CH], ap("rep")[:, 0:CH],
                    ap("tf")[:, i:i + 1], None, ALU.mult, ALU.bypass),
                   inc="vPr")
                nv = cnt["vPr"]
                op(t_, lambda e: e.matmul(
                    pb[2][:, 0:128], ap("iotaf")[:, 0:CH], ap("identf"),
                    start=True, stop=True, is_transpose=True),
                   waits=[("vPr", nv)], inc="tbA")
                opd(v_, lambda e, i=i: e.tensor_copy(
                    bf("TBf")[:, i * 128:(i + 1) * 128], pb[2][:, 0:128]),
                   waits=[("tbA", cnt["tbA"])], inc="vPr")
            if debug:
                for nm2, buf in [("d_dkb", "Dkb"), ("d_dqb", "Dqb"),
                                 ("d_d1b", "D1b"), ("d_tbf", "TBf")]:
                    op(y_, lambda e, nm2=nm2, buf=buf: e.dma_start(
                        out=dbg[nm2], in_=bf(buf)),
                       waits=[("vPr", cnt["vPr"])], inc="yst", amt=16)
                for nm2, buf in [("d_scn", "scn"), ("d_rep", "rep"),
                                 ("d_zf", "zf")]:
                    op(y_, lambda e, nm2=nm2, buf=buf: e.dma_start(
                        out=dbg[nm2], in_=ap(buf)),
                       waits=[("vPr", cnt["vPr"])], inc="yst", amt=16)
            barrier()
            tbA0 = cnt["tbA"]
            if init_only:
                op(y_, lambda e: e.nop(), waits=[("yst", cnt["yst"])])
                barrier()
                yield "done"
                return

            PCW1 = min(1024, FF)
            PCW2 = min(1024, D)
            dq_pieces = []
            for g in range(NT):
                for pc in range(FF // PCW1):
                    dq_pieces.append((1, g, pc))
            for g in range(FF // 128):
                dq_pieces.append((2, g, 0))
            n_dq = len(dq_pieces)

            def dq_srcs(j):
                which, gidx, pc = dq_pieces[j]
                if which == 1:
                    W_ = PCW1
                    return (W_, q1[gidx * 128:(gidx + 1) * 128,
                                   pc * W_:(pc + 1) * W_],
                            s1r[gidx, :, pc * W_:(pc + 1) * W_],
                            w1_sc[gidx * 128:(gidx + 1) * 128,
                                  pc * W_:(pc + 1) * W_])
                W_ = PCW2
                return (W_, q2[gidx * 128:(gidx + 1) * 128, :],
                        s2r[gidx, :, :],
                        w2_sc[gidx * 128:(gidx + 1) * 128, :])

            def emit_dq_slot(u):
                if u >= 1:
                    W_, _, _, wdst = dq_srcs(u - 1)
                    op(v_, lambda e, W_=W_: e.scalar_tensor_tensor(
                        bf("dq_w")[:, 0:W_], ap("dq_q")[:, 0:W_], -128.0,
                        ap("dq_s")[:, 0:W_], ALU.add, ALU.mult),
                       waits=[("gq", cnt["gq"]), ("yq", cnt["yq"])],
                       inc="vPr")
                    op(y_, lambda e, wdst=wdst, W_=W_: e.dma_start(
                        out=wdst, in_=bf("dq_w")[:, 0:W_]),
                       waits=[("vPr", cnt["vPr"])], inc="yq", amt=16)
                if u < n_dq:
                    W_, qsrc, ssrc, _ = dq_srcs(u)
                    op(g_, lambda e, qsrc=qsrc, W_=W_: e.dma_start(
                        out=ap("dq_q")[:, 0:W_], in_=qsrc),
                       waits=[("vPr", cnt["vPr"])], inc="gq", amt=16)
                    op(g_, lambda e, ssrc=ssrc, W_=W_: e.dma_start(
                        out=ap("dq_s")[:, 0:W_], in_=ssrc), inc="gq", amt=16)

            a_done = []
            for s in range(NS):
                c0 = s * SLAB
                xbu = "xb" if s % 2 == 0 else "xb2"
                op(g_, lambda e, c0=c0, xbu=xbu: e.dma_start(
                    out=fm3(bf(xbu), NT), in_=fm(xT, c0, SLAB)),
                   waits=(a_done[s - 2] if s >= 2 else []), inc="gx", amt=16)
                gx_now = cnt["gx"]

                for i in range(NT):
                    bank = i % 2
                    gt = s * NT + i
                    waits = [("gx", gx_now)]
                    if gt >= 2:
                        waits.append(("vkA", gt - 1))
                    if s > 0 and i < 2:
                        waits.append(("vvA", s * CPS * NVH))
                    for k in range(NT):
                        last = k == NT - 1
                        op(t_, lambda e, i=i, k=k, bank=bank, last=last,
                           xbu=xbu: e.matmul(
                            pb[bank][:, 0:SLAB],
                            bf("wk")[:, k * D + i * 128:k * D + (i + 1) * 128],
                            bf(xbu)[:, k * SLAB:(k + 1) * SLAB],
                            start=(k == 0), stop=last),
                           waits=(waits if k == 0 else ()),
                           inc=("tkA" if last else None))
                    kw = [("tkA", cnt["tkA"])]
                    if s > 0 and i == 0:
                        kw.append(("yst", 16 * 4 * s))
                    op(v_, lambda e, i=i, bank=bank: e.tensor_copy(
                        bf("kf")[:, i * SLAB:(i + 1) * SLAB], pb[bank][:, 0:SLAB]),
                       waits=kw, inc="vkA")
                for i in range(NT):
                    bank = 2 + i % 2
                    gt = s * NT + i
                    waits = [("gx", gx_now)]
                    if gt >= 2:
                        waits.append(("srA", gt - 1))
                    if s > 0 and i < 2:
                        waits.append(("vbA", s * CPS * NT))
                    for k in range(NT):
                        last = k == NT - 1
                        op(t_, lambda e, i=i, k=k, bank=bank, last=last,
                           xbu=xbu: e.matmul(
                            pb[bank][:, 0:SLAB],
                            bf("wr")[:, k * D + i * 128:k * D + (i + 1) * 128],
                            bf(xbu)[:, k * SLAB:(k + 1) * SLAB],
                            start=(k == 0), stop=last),
                           waits=(waits if k == 0 else ()),
                           inc=("trA" if last else None))
                    rw = [("trA", cnt["trA"])]
                    if s > 0 and i == 0:
                        rw.append(("yst", 16 * 4 * s))
                    op(s_, lambda e, i=i, bank=bank: e.activation(
                        bf("rf")[:, i * SLAB:(i + 1) * SLAB], pb[bank][:, 0:SLAB],
                        AF.Sigmoid), waits=rw, inc="srA")
                vk_done = cnt["vkA"]
                for cc in range(CPS):
                    for half in range(NVH):
                        bank = half % 2
                        gtv = (s * CPS + cc) * NVH + half
                        waits = [("gx", gx_now), ("vkA", vk_done)]
                        if gtv >= 2:
                            waits.append(("vvA", gtv - 1))
                        for k in range(NT):
                            last = k == NT - 1
                            op(t_, lambda e, cc=cc, half=half, k=k, bank=bank,
                               last=last, xbu=xbu: e.matmul(
                                   pb[bank][:, 0:VH],
                                   bf(xbu)[:, k * SLAB + cc * CH:
                                           k * SLAB + (cc + 1) * CH],
                                   bf("wv")[:, k * D + half * VH:
                                            k * D + (half + 1) * VH],
                                   start=(k == 0), stop=last),
                               waits=(waits if k == 0 else ()),
                               inc=("tvA" if last else None))
                        vw = [("tvA", cnt["tvA"])]
                        if s > 0 and cc == 0 and half == 0:
                            vw.append(("yst", 16 * 4 * s))
                        op(v_, lambda e, cc=cc, half=half, bank=bank:
                           e.tensor_copy(
                               bf("vt")[:, cc * D + half * VH:
                                        cc * D + (half + 1) * VH],
                               pb[bank][:, 0:VH]),
                           waits=vw, inc="vvA")
                a_done.append([("tkA", cnt["tkA"]), ("trA", cnt["trA"]),
                               ("tvA", cnt["tvA"])])
                kf_done = cnt["vkA"]
                vt_done = cnt["vvA"]
                srA_done = cnt["srA"]
                for cc in range(CPS):
                    kglob = s * CPS + cc
                    cur = "Sf0" if kglob % 2 == 0 else "Sf1"
                    nxt = "Sf1" if kglob % 2 == 0 else "Sf0"
                    for p in range(NT):
                        bank = 2 + p % 2
                        gtb = kglob * NT + p
                        waits = [("vkA", kf_done), ("srA", srA_done)]
                        if gtb >= 2:
                            waits.append(("vbA", gtb - 1))
                        op(t_, lambda e, p=p, cc=cc, bank=bank: e.matmul(
                            pb[bank][:, 0:64].bitcast(BF16),
                            bf("kf")[:, p * SLAB + cc * CH:p * SLAB + (cc + 1) * CH],
                            bf("identb"), start=True, stop=True,
                            is_transpose=True),
                           waits=waits, inc="tbA")
                        bw = [("tbA", cnt["tbA"])]
                        if s > 0 and cc == 0 and p == 0:
                            bw.append(("yst", 16 * 4 * s))
                        op(v_, lambda e, p=p, cc=cc, bank=bank: e.tensor_tensor(
                            bf("Bt")[:, cc * D + p * 128:cc * D + (p + 1) * 128],
                            pb[bank][:, 0:64].bitcast(BF16),
                            bf("TBf")[:, p * 128:(p + 1) * 128], ALU.mult),
                           waits=bw, inc="vbA")
                    for p in range(NT):
                        gtu = kglob * NT + p
                        waits = [("vbA", cnt["vbA"]), ("vvA", vt_done)]
                        if gtu >= 1:
                            waits.append(("vsA", gtu))
                        for P in range(HPT):
                            hcol = p * 128 + P * HD
                            op(t_, lambda e, cc=cc, P=P, hcol=hcol: e.matmul(
                                pb[5][P * HD:(P + 1) * HD, 0:HD],
                                bf("Bt")[:, cc * D + hcol:cc * D + hcol + HD],
                                bf("vt")[:, cc * D + hcol:cc * D + hcol + HD],
                                start=True, stop=True,
                                tile_position=(0, P * HD),
                                skip_group_check=(P > 0)),
                               waits=(waits if P == 0 else ()),
                               inc=("tuA" if P == HPT - 1 else None))
                        op(v_, lambda e, p=p, cur=cur, nxt=nxt:
                           e.scalar_tensor_tensor(
                               ap(nxt)[:, p * HD:(p + 1) * HD],
                               ap(cur)[:, p * HD:(p + 1) * HD],
                               ap("dec128")[:, p:p + 1],
                               pb[5][:, 0:HD], ALU.mult, ALU.add),
                           waits=[("tuA", cnt["tuA"])], inc="vsA")
                op(y_, lambda e, c0=c0: e.dma_start(
                    out=fm(k_sc, c0, SLAB), in_=fm3(bf("kf"), NT)),
                   waits=[("vkA", cnt["vkA"])], inc="yst", amt=16)
                op(y_, lambda e, c0=c0: e.dma_start(
                    out=fm(r_sc, c0, SLAB), in_=fm3(bf("rf"), NT)),
                   waits=[("srA", cnt["srA"])], inc="yst", amt=16)
                op(y_, lambda e, c0=c0: e.dma_start(
                    out=tm(v_sc, c0, SLAB), in_=fm3(bf("vt"), CPS)),
                   waits=[("vvA", cnt["vvA"])], inc="yst", amt=16)
                op(y_, lambda e, c0=c0: e.dma_start(
                    out=tm(b_sc, c0, SLAB), in_=fm3(bf("Bt"), CPS)),
                   waits=[("vbA", cnt["vbA"])], inc="yst", amt=16)
            barrier()

            op(y_, lambda e: e.dma_start(out=ccin, in_=ap("Sf0")),
               inc="yst", amt=16)
            op(g_, lambda e: e.collective_compute(
                "AllGather", ALU.bypass,
                replica_groups=[[0, 1], [2, 3], [4, 5], [6, 7]],
                ins=[ccin.opt()], outs=[ccout.opt()]),
               waits=[("yst", cnt["yst"])], inc="cc")
            op(g_, lambda e: e.dma_start(out=ap("Sin"), in_=ccout[0]),
               waits=[("cc", cnt["cc"])], inc="gsin", amt=16)
            op(v_, lambda e: e.tensor_scalar(
                ap("Sf0"), ap("Sin"), ap("flag")[:, 0:1], None,
                ALU.mult, ALU.bypass),
               waits=[("gsin", cnt["gsin"])], inc="vS1")
            barrier()
            yield "B1"

            op(g_, lambda e: e.dma_start(
                out=fm3(bf("wo"), NT), in_=woT.rearrange("(k p) n -> p k n", p=128)),
               inc="gwo", amt=16)
            wo_ld = cnt["gwo"]
            op(v_, lambda e: e.memset(pb[2][:, 0:512], 0.0), inc="vPr")
            op(v_, lambda e: e.memset(pb[3][:, 0:512], 0.0), inc="vPr")
            op(v_, lambda e: e.memset(pb[4][:, 0:512], 0.0), inc="vPr")
            op(v_, lambda e: e.memset(pb[0][:, 0:512], 0.0), inc="vPr")
            op(v_, lambda e: e.memset(bf("Sb"), 0.0), inc="vPr")

            for s in range(NS):
                c0 = s * SLAB
                ld_wait = []
                if s > 0:
                    ld_wait = [("tU1", cnt["tU1"]), ("tY", cnt["tY"]),
                               ("vPre", cnt["vPre"]), ("vPr", cnt["vPr"])]
                op(s_, lambda e, c0=c0: e.dma_start(
                    out=fm3(bf("kf"), NT), in_=fm(k_sc, c0, SLAB)),
                   waits=ld_wait + [("yst", 16 * (4 * NS + 1))],
                   inc="sld", amt=16)
                op(s_, lambda e, c0=c0: e.dma_start(
                    out=fm3(bf("rf"), NT), in_=fm(r_sc, c0, SLAB)), inc="sld", amt=16)
                op(s_, lambda e, c0=c0: e.dma_start(
                    out=fm3(bf("vt"), CPS), in_=tm(v_sc, c0, SLAB)), inc="sld", amt=16)
                op(s_, lambda e, c0=c0: e.dma_start(
                    out=fm3(bf("Bt"), CPS), in_=tm(b_sc, c0, SLAB)), inc="sld", amt=16)
                op(s_, lambda e, c0=c0: e.dma_start(
                    out=fm3(ap("xf"), NT), in_=fm(xT, c0, SLAB)), inc="sld", amt=16)
                sld_now = cnt["sld"]

                for cc in range(CPS):
                    kglob = s * CPS + cc
                    emit_dq_slot(2 * kglob)
                    cur = "Sf0" if kglob % 2 == 0 else "Sf1"
                    nxt = "Sf1" if kglob % 2 == 0 else "Sf0"
                    pw = [("sld", sld_now), ("vS1", cnt["vS1"])]
                    if kglob > 0:
                        pw += [("tY", cnt["tY"]), ("tA1", cnt["tA1"]),
                               ("tU1", cnt["tU1"])]

                    def rch3():
                        return bf("rf").rearrange("p (i c) -> p i c", i=NT)[
                            :, :, cc * CH:(cc + 1) * CH]

                    def kch3():
                        return bf("kf").rearrange("p (i c) -> p i c", i=NT)[
                            :, :, cc * CH:(cc + 1) * CH]

                    op(v_, lambda e: e.tensor_tensor(
                        bf("qt").rearrange("p (i c) -> p i c", i=NT),
                        rch3(),
                        bf("D1b").rearrange("p (i c) -> p i c", i=NT),
                        ALU.mult), waits=pw, inc="vPr")
                    for d in range(4):
                        op(v_, lambda e, d=d: e.tensor_tensor(
                            bf("Qh")[:, d * CHW:(d + 1) * CHW]
                            .rearrange("p (i c) -> p i c", i=NT),
                            rch3(),
                            bf("Dqb")[:, d * CHW:(d + 1) * CHW]
                            .rearrange("p (i c) -> p i c", i=NT),
                            ALU.mult), inc="vPr")
                    op(v_, lambda e: e.tensor_tensor(
                        bf("kd").rearrange("p (i c) -> p i c", i=NT),
                        kch3(),
                        bf("Dkb").rearrange("p (i c) -> p i c", i=NT),
                        ALU.mult), inc="vPr")
                    op(v_, lambda e, cur=cur: e.tensor_copy(
                        bf("Sb").rearrange("p (i c) -> p i c", i=NT)
                        [0:HD, :, 0:HD],
                        ap(cur).rearrange("p (i c) -> p i c", i=NT)
                        [0:HD, :, :]), inc="vPr")
                    op(v_, lambda e, cur=cur: e.tensor_copy(
                        bf("Sb").rearrange("p (i c) -> p i c", i=NT)
                        [HD:128, :, HD:CH],
                        ap(cur).rearrange("p (i c) -> p i c", i=NT)
                        [HD:128, :, :]), inc="vPr")
                    prep_done = cnt["vPr"]

                    for p in range(NT):
                        for P in range(HPT):
                            bank = 2 + P
                            gA = (kglob * NT + p) * HPT + P
                            wA = [("vPr", prep_done)]
                            if gA >= 2:
                                wA.append(("sAe", gA - 1))
                            tiles = []
                            for J in range(NB):
                                for M in range(J):
                                    tiles.append((M * SUB, SUB, J * SUB, SUB,
                                                  J - M))
                                tiles.append((J * SUB, SUB // 2, J * SUB,
                                              SUB // 2, 0))
                                tiles.append((J * SUB, SUB, J * SUB + SUB // 2,
                                              SUB // 2, 0))
                            for ti, (m0, msz, j0, jsz, dta) in enumerate(tiles):
                                first = ti == 0
                                last = ti == len(tiles) - 1
                                op(t_, lambda e, p=p, P=P, m0=m0, msz=msz,
                                   j0=j0, jsz=jsz, dta=dta, bank=bank,
                                   first=first: e.matmul(
                                       pb[bank][m0:m0 + msz, j0:j0 + jsz],
                                       bf("kd")[P * HD:(P + 1) * HD,
                                                p * CH + m0:p * CH + m0 + msz],
                                       bf("Qh")[P * HD:(P + 1) * HD,
                                                dta * CHW + p * CH + j0:
                                                dta * CHW + p * CH + j0 + jsz],
                                       start=True, stop=True,
                                       tile_position=(P * HD, m0),
                                       skip_group_check=(not first)),
                                   waits=(wA if first else ()),
                                   inc=("tA1" if last else None))
                            Atp = "At" if p % 2 == 0 else "At2"
                            gg = kglob * NT + p
                            op(s_, lambda e, P=P, Atp=Atp, bank=bank: e.copy(
                                bf(Atp)[:, (HPT + P) * CH:(HPT + P + 1) * CH],
                                pb[bank][:, 0:CH]),
                               waits=[("tA1", cnt["tA1"]),
                                      ("vA1", 2 * (gg - 2) + P + 1)],
                               inc="sAe")
                            op(v_, lambda e, P=P, Atp=Atp: e.tensor_tensor(
                                bf(Atp)[:, P * CH:(P + 1) * CH],
                                bf(Atp)[:, (HPT + P) * CH:(HPT + P + 1) * CH],
                                bf("trimask"), ALU.mult),
                               waits=[("sAe", cnt["sAe"]),
                                      ("tY", gg - 1)], inc="vA1")
                        gY = kglob * NT + p
                        bY = 4 if p % 2 == 0 else 0
                        wY = [("vPr", prep_done), ("vA1", cnt["vA1"])]
                        if gY >= 2:
                            wY.append(("sY", gY - 1))
                        op(t_, lambda e, p=p, bY=bY: e.matmul(
                            pb[bY][:, 0:CH], bf("Sb")[:, p * CH:(p + 1) * CH],
                            bf("qt")[:, p * CH:(p + 1) * CH],
                            start=True, stop=True), waits=wY)
                        for P in range(HPT):
                            hcol = p * 128 + P * HD
                            Atp2 = "At" if p % 2 == 0 else "At2"
                            op(t_, lambda e, P=P, hcol=hcol, cc=cc,
                               Atp2=Atp2, bY=bY: e.matmul(
                                pb[bY][P * HD:(P + 1) * HD, 0:CH],
                                bf("vt")[:, cc * D + hcol:cc * D + hcol + HD],
                                bf(Atp2)[:, P * CH:(P + 1) * CH],
                                start=False, stop=True,
                                tile_position=(0, P * HD),
                                skip_group_check=True),
                               inc=("tY" if P == HPT - 1 else None))
                        op(s_, lambda e, p=p, cc=cc, bY=bY: e.copy(
                            bf("yb")[:, p * SLAB + cc * CH:
                                     p * SLAB + (cc + 1) * CH],
                            pb[bY][:, 0:CH]),
                           waits=[("tY", cnt["tY"]), ("tWo", cnt["tWo"])],
                           inc="sY")
                        if debug and s == 0 and cc == 0 and p == NT - 1:
                            op(y_, lambda e: e.dma_start(
                                out=dbg["d_at"], in_=bf("At")),
                               waits=[("vA1", cnt["vA1"])], inc="yst", amt=16)
                            op(y_, lambda e: e.dma_start(
                                out=dbg["d_qh"], in_=bf("Qh")),
                               waits=[("vA1", cnt["vA1"])], inc="yst", amt=16)
                            op(y_, lambda e: e.dma_start(
                                out=dbg["d_kd"], in_=bf("kd")),
                               waits=[("vA1", cnt["vA1"])], inc="yst", amt=16)
                            op(y_, lambda e: e.dma_start(
                                out=dbg["d_qt"], in_=bf("qt")),
                               waits=[("vA1", cnt["vA1"])], inc="yst", amt=16)
                            op(y_, lambda e: e.dma_start(
                                out=dbg["d_sb"], in_=bf("Sb")),
                               waits=[("vA1", cnt["vA1"])], inc="yst", amt=16)
                        gU = kglob * NT + p
                        wU = [("sld", sld_now), ("vS1", cnt["vS1"])]
                        for P in range(HPT):
                            hcol = p * 128 + P * HD
                            op(t_, lambda e, P=P, hcol=hcol, cc=cc: e.matmul(
                                pb[5][P * HD:(P + 1) * HD, 0:HD],
                                bf("Bt")[:, cc * D + hcol:cc * D + hcol + HD],
                                bf("vt")[:, cc * D + hcol:cc * D + hcol + HD],
                                start=True, stop=True,
                                tile_position=(0, P * HD),
                                skip_group_check=(P > 0)),
                               waits=(wU if P == 0 else ()),
                               inc=("tU1" if P == HPT - 1 else None))
                        op(v_, lambda e, p=p, cur=cur, nxt=nxt:
                           e.scalar_tensor_tensor(
                               ap(nxt)[:, p * HD:(p + 1) * HD],
                               ap(cur)[:, p * HD:(p + 1) * HD],
                               ap("dec128")[:, p:p + 1],
                               pb[5][:, 0:HD], ALU.mult, ALU.add),
                           waits=[("tU1", cnt["tU1"])], inc="vS1")
                    emit_dq_slot(2 * kglob + 1)

                sy_done = cnt["sY"]
                for i in range(NT):
                    bank = i % 2
                    gt = s * NT + i
                    waits = [("sY", sy_done), ("sld", sld_now),
                             ("gwo", wo_ld), ("vLN", cnt["vLN"])]
                    if gt >= 2:
                        waits.append(("vPre", gt - 1))
                    for k in range(NT):
                        last = k == NT - 1
                        op(t_, lambda e, i=i, k=k, bank=bank, last=last:
                           e.matmul(
                               pb[bank][:, 0:SLAB],
                               bf("wo")[:, k * D + i * 128:k * D + (i + 1) * 128],
                               bf("yb")[:, k * SLAB:(k + 1) * SLAB],
                               start=(k == 0), stop=last),
                           waits=(waits if k == 0 else ()),
                           inc=("tWo" if last else None))
                    op(v_, lambda e, i=i, bank=bank: e.scalar_tensor_tensor(
                        ap("pre1")[:, i * SLAB:(i + 1) * SLAB],
                        pb[bank][:, 0:SLAB], 1.0,
                        ap("xf")[:, i * SLAB:(i + 1) * SLAB],
                        ALU.mult, ALU.add),
                       waits=[("tWo", cnt["tWo"])], inc="vPre")
                if debug:
                    op(y_, lambda e, c0=c0: e.dma_start(
                        out=fm(dbg["d_pre"], c0, SLAB), in_=fm3(ap("pre1"), NT)),
                       waits=[("vPre", cnt["vPre"])], inc="yst", amt=16)
                    op(y_, lambda e, c0=c0: e.dma_start(
                        out=fm(dbg["d_yb"], c0, SLAB), in_=fm3(bf("yb"), NT)),
                       waits=[("sY", cnt["sY"])], inc="yst", amt=16)
                ln_block("pre1", "x1", None, None, None, EPS,
                         "tE", "vLN", "sSq", [("vPre", cnt["vPre"])],
                         e1b=0, e2b=1)
                op(y_, lambda e, c0=c0: e.dma_start(
                    out=fm(x1_sc, c0, SLAB), in_=fm3(ap("x1"), NT)),
                   waits=[("vLN", cnt["vLN"])], inc="yx1", amt=16)
            for u in range(2 * NS * CPS, n_dq + 1):
                emit_dq_slot(u)
            barrier()
            yield "B2h0"

            for half in range(2):
                sldn = "sld2" if half == 0 else "sld3"
                tf1 = "tF1" if half == 0 else "tG1"
                sf1 = "sF1" if half == 0 else "sG1"
                tf2 = "tF2" if half == 0 else "tG2"
                vf2 = "vF2" if half == 0 else "vG2"
                vlnn = "vLN2" if half == 0 else "vLN3"
                z_incs = 1 if half == 0 else 3
                wwait = [("yq", cnt["yq"]), ("yq2", cnt["yq2"])]
                if half == 1:
                    wwait += [("tF1", cnt["tF1"]), ("tF2", cnt["tF2"])]
                op(s_, lambda e, half=half: e.dma_start(
                    out=fm3(bf("w1h"), NT),
                    in_=w1_sc[:, half * FFH:(half + 1) * FFH]
                    .rearrange("(k p) n -> p k n", p=128)),
                   waits=wwait, inc=sldn, amt=16)
                op(s_, lambda e, half=half: e.dma_start(
                    out=fm3(bf("w2h"), NKH),
                    in_=w2_sc[half * FFH:(half + 1) * FFH, :]
                    .rearrange("(k p) n -> p k n", p=128)),
                   inc=sldn, amt=16)
                snaps = []
                for s2 in range(NS):
                    c0 = s2 * SLAB
                    par = s2 % 2
                    x1u = "x1" if par == 0 else "x1b"
                    z1u = "z1" if par == 0 else "z1b"
                    mu_ = "m0" if par == 0 else "m1"
                    ldw = []
                    if s2 >= 2:
                        sn = snaps[s2 - 2]
                        ldw = [(tf1, sn[0]), (tf2, sn[1]),
                               (vlnn, sn[2]), (vf2, sn[3]),
                               ("sSq2", sn[4])]
                        if half == 0:
                            ldw += [("yz1", sn[5]), ("yh", sn[6])]
                        else:
                            ldw += [("yout", sn[7])]
                    op(s_, lambda e, c0=c0, x1u=x1u: e.dma_start(
                        out=fm3(ap(x1u), NT), in_=fm(x1_sc, c0, SLAB)),
                       waits=ldw + [("yx1", 16 * NS)], inc=sldn, amt=16)
                    if half == 1:
                        hbw = [("yh", cnt["yh"])]
                        if s2 >= 1:
                            hbw.append((tf1, snaps[s2 - 1][0]))
                        op(s_, lambda e, c0=c0: e.dma_start(
                            out=fm3(bf("hb"), NT), in_=fm(h_sc, c0, SLAB)),
                           waits=hbw, inc=sldn, amt=16)
                        op(s_, lambda e, c0=c0, z1u=z1u: e.dma_start(
                            out=fm3(ap(z1u), NT), in_=fm(z1_sc, c0, SLAB)),
                           waits=[("yz1", cnt["yz1"])], inc=sldn, amt=16)
                    sld_now2 = cnt[sldn]

                    if half == 0:
                        lnw = [(sldn, sld_now2)]
                        if s2 >= 1:
                            lnw.append(("yh", snaps[s2 - 1][6]))
                        ln_block(x1u, None, "hb", "gam", "bet", EPS,
                                 "tE", "vLN2", "sSq2", lnw)
                        op(y_, lambda e, c0=c0: e.dma_start(
                            out=fm(h_sc, c0, SLAB), in_=fm3(bf("hb"), NT)),
                           waits=[("vLN2", cnt["vLN2"])], inc="yh", amt=16)
                        hb_ready = [("vLN2", cnt["vLN2"])]
                    else:
                        hb_ready = [(sldn, sld_now2)]
                    for oh in range(NOH):
                        bank = oh % 2
                        gt = s2 * NOH + oh
                        waits = list(hb_ready) + [(sldn, sld_now2)]
                        if gt >= 2:
                            waits.append((sf1, gt - 1))
                        for k in range(NT):
                            last = k == NT - 1
                            op(t_, lambda e, oh=oh, k=k, bank=bank, last=last:
                               e.matmul(
                                   pb[bank][:, 0:SLAB],
                                   bf("w1h")[:, k * FFH + oh * 128:
                                             k * FFH + (oh + 1) * 128],
                                   bf("hb")[:, k * SLAB:(k + 1) * SLAB],
                                   start=(k == 0), stop=last),
                               waits=(waits if k == 0 else ()),
                               inc=(tf1 if last else None))
                        evw = [(tf1, cnt[tf1])]
                        if half == 1 and oh == 0 and s2 >= 1:
                            evw.append(("yout", snaps[s2 - 1][7]))
                        op(s_, lambda e, oh=oh, bank=bank, half=half:
                           e.activation(
                               bf("ab")[:, oh * SLAB:(oh + 1) * SLAB],
                               pb[bank][:, 0:SLAB], AF.Silu,
                               bias=ap("b1")[:, half * NOH + oh:
                                             half * NOH + oh + 1]),
                           waits=evw, inc=sf1)
                    for i in range(NT):
                        bank = 2 + i % 2
                        gt = s2 * NT + i
                        waits = [(sf1, cnt[sf1])]
                        if gt >= 2:
                            waits.append((vf2, z_incs * (gt - 1) + 1))
                        for k in range(NKH):
                            last = k == NKH - 1
                            op(t_, lambda e, i=i, k=k, bank=bank, last=last:
                               e.matmul(
                                   pb[bank][:, 0:SLAB],
                                   bf("w2h")[:, k * D + i * 128:
                                             k * D + (i + 1) * 128],
                                   bf("ab")[:, k * SLAB:(k + 1) * SLAB],
                                   start=(k == 0), stop=last),
                               waits=(waits if k == 0 else ()),
                               inc=(tf2 if last else None))
                        if half == 0:
                            op(v_, lambda e, i=i, bank=bank, z1u=z1u:
                               e.tensor_copy(
                                   ap(z1u)[:, i * SLAB:(i + 1) * SLAB],
                                   pb[bank][:, 0:SLAB]),
                               waits=[(tf2, cnt[tf2])], inc=vf2)
                        else:
                            op(v_, lambda e, i=i, bank=bank, z1u=z1u:
                               e.tensor_tensor(
                                   ap("tz"),
                                   ap(z1u)[:, i * SLAB:(i + 1) * SLAB],
                                   pb[bank][:, 0:SLAB], ALU.add),
                               waits=[(tf2, cnt[tf2])], inc=vf2)
                            op(v_, lambda e, i=i: e.tensor_scalar(
                                ap("tz2"), ap("tz"), ap("b2")[:, i:i + 1],
                                0.25, ALU.add, ALU.mult), inc=vf2)
                            op(v_, lambda e, i=i, mu_=mu_, x1u=x1u:
                               e.tensor_tensor(
                                   ap(mu_)[:, i * SLAB:(i + 1) * SLAB],
                                   ap("tz2"),
                                   ap(x1u)[:, i * SLAB:(i + 1) * SLAB],
                                   ALU.add),
                               inc=vf2)
                    if half == 0:
                        op(y_, lambda e, c0=c0, z1u=z1u: e.dma_start(
                            out=fm(z1_sc, c0, SLAB), in_=fm3(ap(z1u), NT)),
                           waits=[("vF2", cnt["vF2"])], inc="yz1", amt=16)
                    else:
                        ln_block(mu_, "outb", None, None, None, EPS / 4.0,
                                 "tE", "vLN3", "sSq2", [("vG2", cnt["vG2"])])
                        op(y_, lambda e, c0=c0: e.dma_start(
                            out=fm(outT, c0, SLAB), in_=fm3(ap("outb"), NT)),
                           waits=[("vLN3", cnt["vLN3"])], inc="yout", amt=16)
                    snaps.append((cnt[tf1], cnt[tf2], cnt[vlnn], cnt[vf2],
                                  cnt["sSq2"], cnt["yz1"], cnt["yh"],
                                  cnt["yout"]))
                if half == 0:
                    barrier()
                    yield "B2h1"

            if debug:
                for nm, srcd in [("d_k", k_sc), ("d_r", r_sc), ("d_v", v_sc),
                                 ("d_b", b_sc), ("d_x1", x1_sc), ("d_h", h_sc),
                                 ("d_z1", z1_sc), ("d_cc", ccin),
                                 ("d_w1", w1_sc), ("d_w2", w2_sc)]:
                    op(y_, lambda e, nm=nm, srcd=srcd: e.dma_start(
                        out=dbg[nm], in_=srcd),
                       waits=[("yout", cnt["yout"]), ("yq", cnt["yq"]),
                              ("yq2", cnt["yq2"]),
                              ("yst", cnt["yst"]), ("yx1", cnt["yx1"]),
                              ("yh", cnt["yh"]), ("yz1", cnt["yz1"])],
                       inc="yout", amt=16)
            op(y_, lambda e: e.nop(),
               waits=[("yout", cnt["yout"]), ("yst", cnt["yst"]),
                      ("yq", cnt["yq"]), ("yq2", cnt["yq2"]),
                      ("yx1", cnt["yx1"]),
                      ("yh", cnt["yh"]), ("yz1", cnt["yz1"])])
            barrier()
            yield "done"

        block = E(nc.Block())

        def run_walk(name, handle):
            for tag in walk(name, handle):
                if tag == "done":
                    break

        @block.gpsimd
        def _(g):
            run_walk("gpsimd", g)

        @block.tensor
        def _(t):
            run_walk("tensor", t)

        @block.vector
        def _(v):
            run_walk("vector", v)

        @block.scalar
        def _(s):
            run_walk("scalar", s)

        @block.sync
        def _(y):
            run_walk("sync", y)

    return nc



def _stage(arr, cfg):
    def colmajor(v, nt):
        return np.ascontiguousarray(np.asarray(v, np.float32).reshape(nt, 128).T)
    out = {
        "wkT": np.ascontiguousarray(np.asarray(arr["Wk"], np.float32).T),
        "wvT": np.ascontiguousarray(np.asarray(arr["Wv"], np.float32).T),
        "wrT": np.ascontiguousarray(np.asarray(arr["Wr"], np.float32).T),
        "woT": np.ascontiguousarray(np.asarray(arr["Wo"], np.float32).T),
        "decf": colmajor(np.asarray(arr["decay"]).reshape(-1), cfg.NT),
        "tff": colmajor(np.asarray(arr["tf"]).reshape(-1), cfg.NT),
        "gamf": colmajor(arr["gam"], cfg.NT),
        "betf": colmajor(arr["bet"], cfg.NT),
        "b1f": colmajor(arr["b1"], cfg.FF // 128),
        "b2f": colmajor(arr["b2"], cfg.NT),
        "q1": np.ascontiguousarray(np.asarray(arr["q1"], np.int32)),
        "q2": np.ascontiguousarray(np.asarray(arr["q2"], np.int32)),
        "s1r": np.ascontiguousarray(np.broadcast_to(
            np.asarray(arr["s1"], np.float32)[:, None, :],
            (cfg.D // 128, 128, cfg.FF))),
        "s2r": np.ascontiguousarray(np.broadcast_to(
            np.asarray(arr["s2"], np.float32)[:, None, :],
            (cfg.FF // 128, 128, cfg.D))),
    }
    return out


def kernel(L0_attn_x, L0_attn_Wk, L0_attn_Wv, L0_attn_Wr, L0_attn_Wo,
           L0_attn_time_decay, L0_attn_time_first,
           L0_ffn_ln_gamma, L0_ffn_ln_beta,
           L0_ffn_fc1_q, L0_ffn_fc1_scales, L0_ffn_fc1_bias,
           L0_ffn_fc2_q, L0_ffn_fc2_scales, L0_ffn_fc2_bias):
    cfg = Cfg()
    x = np.asarray(L0_attn_x, np.float32)
    B, L, D = x.shape
    TOK = cfg.TOK
    nc = build_program(cfg)
    common = _stage({
        "Wk": L0_attn_Wk, "Wv": L0_attn_Wv, "Wr": L0_attn_Wr, "Wo": L0_attn_Wo,
        "decay": L0_attn_time_decay, "tf": L0_attn_time_first,
        "gam": L0_ffn_ln_gamma, "bet": L0_ffn_ln_beta,
        "q1": L0_ffn_fc1_q, "s1": L0_ffn_fc1_scales, "b1": L0_ffn_fc1_bias,
        "q2": L0_ffn_fc2_q, "s2": L0_ffn_fc2_scales, "b2": L0_ffn_fc2_bias,
    }, cfg)

    in_maps = []
    for core in range(8):
        b, h = core // 2, core % 2
        m = dict(common)
        m["xT"] = np.ascontiguousarray(x[b, h * TOK:(h + 1) * TOK, :].T)
        m["flagf"] = np.full((128, 1), float(h), np.float32)
        in_maps.append(m)

    res = run_bass_kernel_spmd(nc, in_maps, core_ids=list(range(8)))

    out = np.empty((B, L, D), np.float32)
    for core in range(8):
        b, h = core // 2, core % 2
        out[b, h * TOK:(h + 1) * TOK, :] = np.asarray(
            res.results[core]["outT"], np.float32).T
    return out
